# version 09
# baseline (speedup 1.0000x reference)
import math
import numpy as np
import concourse.bass as bass
import concourse.mybir as mybir
from concourse.bass_utils import run_bass_kernel_spmd
from contextlib import ExitStack

F32 = mybir.dt.float32
BF16 = mybir.dt.bfloat16
AF = mybir.ActivationFunctionType
ALU = mybir.AluOpType
AX = mybir.AxisListType

D, T, H, HD, DL = 1024, 2048, 16, 64, 256
LAM = 0.01
NS, NT, NJ = 8, 16, 4
E0, I0, E1, F1 = 16, 1024, 8, 1024
ENGINES = ("sync", "scalar", "vector", "tensor", "gpsimd")


class Prog:
    def __init__(self, nc):
        self.nc = nc
        self.q = {e: [] for e in ENGINES}
        self.cnt = {}
        self.waited = {e: {} for e in ENGINES}
        self.sems = {}
        self.guard = {}

    def _wl(self, eng, waits):
        out = []
        for s, v in waits:
            if not v:
                continue
            if self.waited[eng].get(s, 0) < v:
                self.waited[eng][s] = v
                out.append((s, v))
        return out

    _SELF = {"vector": "dve", "scalar": "act", "gpsimd": "pool"}

    def op(self, eng, fn, wait=(), inc=None, inc_amt=1):
        selfsem = self._SELF.get(eng)
        wait = list(wait)
        if selfsem is not None:
            prev = self.cnt.get(selfsem, 0)
            if prev:
                wait.append((selfsem, prev))
            if inc is None:
                inc, inc_amt = selfsem, 1
        wl = self._wl(eng, wait)
        if inc is not None:
            self.cnt[inc] = self.cnt.get(inc, 0) + inc_amt

        def thunk(h, wl=wl, fn=fn, inc=inc, inc_amt=inc_amt):
            for s, v in wl:
                h.wait_ge(self.sems[s], v)
            r = fn(h)
            if inc is not None:
                r.then_inc(self.sems[inc], inc_amt)
        self.q[eng].append(thunk)
        return self.cnt.get(inc) if inc is not None else None

    def dma(self, eng, out, in_, wait=(), fence=True, **kw):
        sem = "dq_" + eng
        wl = self._wl(eng, wait)
        self.cnt[sem] = self.cnt.get(sem, 0) + 16
        cum = self.cnt[sem]

        def thunk(h, wl=wl, out=out, in_=in_, kw=kw, sem=sem, cum=cum, fence=fence):
            for s, v in wl:
                h.wait_ge(self.sems[s], v)
            h.dma_start(out=out, in_=in_, **kw).then_inc(self.sems[sem], 16)
            if fence:
                h.wait_ge(self.sems[sem], cum)
        self.q[eng].append(thunk)
        if fence:
            self.waited[eng][sem] = max(self.waited[eng].get(sem, 0), cum)
        return cum

    def wait(self, eng, waits):
        wl = self._wl(eng, waits)
        if not wl:
            return

        def thunk(h, wl=wl):
            for s, v in wl:
                h.wait_ge(self.sems[s], v)
        self.q[eng].append(thunk)

    def now(self, sem):
        return self.cnt.get(sem, 0)

    def barrier(self):
        snap = [(s, v) for s, v in self.cnt.items()]
        for e in ENGINES:
            self.wait(e, snap)

    def need(self, key):
        g = self.guard.get(key)
        return [g] if g else []

    def used(self, key, sem, val):
        self.guard[key] = (sem, val)

    def build(self, stack):
        for name in list(self.cnt.keys()):
            self.sems[name] = stack.enter_context(self.nc.semaphore(name))
        block = stack.enter_context(self.nc.Block())
        for eng in ENGINES:
            thunks = self.q[eng]
            if not thunks:
                continue

            def body(h, thunks=thunks):
                for t in thunks:
                    t(h)
            getattr(block, eng)(body)


def _mk_helpers(p):
    def MM(ps, lhsT, rhs, start, stop, wait=()):
        return p.op("tensor", lambda t, ps=ps, l=lhsT, r=rhs, a=start, b=stop:
                    t.matmul(out=ps, lhsT=l, rhs=r, start=a, stop=b),
                    wait=wait, inc=("pe" if stop else None))

    def TRN(ps, src, ident, wait=()):
        return p.op("tensor", lambda t, ps=ps, s=src, i=ident:
                    t.transpose(out=ps, in_=s, identity=i), wait=wait, inc="pe")

    def CP(out, in_, wait=(), eng="vector"):
        sem = {"vector": "dve", "gpsimd": "pool", "scalar": "act"}[eng]
        if eng == "scalar":
            return p.op(eng, lambda h, o=out, i=in_: h.copy(out=o, in_=i),
                        wait=wait, inc=sem)
        return p.op(eng, lambda h, o=out, i=in_: h.tensor_copy(out=o, in_=i),
                    wait=wait, inc=sem)

    def TT(out, a, b, alu, wait=(), eng="vector"):
        sem = {"vector": "dve", "gpsimd": "pool"}[eng]
        return p.op(eng, lambda h, o=out, a=a, b=b, op=alu:
                    h.tensor_tensor(out=o, in0=a, in1=b, op=op), wait=wait, inc=sem)

    def TS(out, in0, s1, s2=None, op0=ALU.mult, op1=ALU.add, wait=(), eng="vector"):
        sem = {"vector": "dve", "gpsimd": "pool"}[eng]
        if s2 is None:
            return p.op(eng, lambda h, o=out, i=in0, s=s1, op=op0:
                        h.tensor_scalar(out=o, in0=i, scalar1=s, scalar2=None, op0=op),
                        wait=wait, inc=sem)
        return p.op(eng, lambda h, o=out, i=in0, a=s1, b=s2, o0=op0, o1=op1:
                    h.tensor_scalar(out=o, in0=i, scalar1=a, scalar2=b, op0=o0, op1=o1),
                    wait=wait, inc=sem)

    def STT(out, in0, sc, in1, op0, op1, wait=(), eng="vector"):
        sem = {"vector": "dve", "gpsimd": "pool"}[eng]
        return p.op(eng, lambda h, o=out, i=in0, s=sc, j=in1, o0=op0, o1=op1:
                    h.scalar_tensor_tensor(out=o, in0=i, scalar=s, in1=j, op0=o0, op1=o1),
                    wait=wait, inc=sem)

    def ACTF(out, in_, func, bias=0.0, scale=1.0, accum=None, wait=()):
        return p.op("scalar", lambda h, o=out, i=in_, f=func, b=bias, s=scale, a=accum:
                    h.activation(out=o, in_=i, func=f, bias=b, scale=s, accum_out=a),
                    wait=wait, inc="act")

    def RED(out, in_, alu, wait=(), negate=None):
        return p.op("vector", lambda h, o=out, i=in_, op=alu, n=negate:
                    h.tensor_reduce(out=o, in_=i, axis=AX.X, op=op, negate=n),
                    wait=wait, inc="dve")

    def SCAN(out, d0, d1, init, wait=()):
        return p.op("vector", lambda h, o=out, a=d0, b=d1, i=init:
                    h.tensor_tensor_scan(out=o, data0=a, data1=b, initial=i,
                                         op0=ALU.mult, op1=ALU.add),
                    wait=wait, inc="dve")

    def SEL(out, mask, tv, fv, wait=()):
        return p.op("vector", lambda h, o=out, m=mask, a=tv, b=fv:
                    h.select(out=o, mask=m, on_true=a, on_false=b), wait=wait, inc="dve")

    def RCP(out, in_, wait=()):
        return p.op("vector", lambda h, o=out, i=in_: h.reciprocal(out=o, in_=i),
                    wait=wait, inc="dve")

    def MSET(ap, val, wait=(), eng="vector"):
        sem = {"vector": "dve", "gpsimd": "pool"}[eng]
        return p.op(eng, lambda h, a=ap, v=val: h.memset(a, v), wait=wait, inc=sem)

    return MM, TRN, CP, TT, TS, STT, ACTF, RED, SCAN, SEL, RCP, MSET


def build_progA(upto="all", debug=()):
    nc = bass.Bass("TRN2", target_bir_lowering=False, debug=False)
    stk = ExitStack()
    p = Prog(nc)
    MM, TRN, CP, TT, TS, STT, ACTF, RED, SCAN, SEL, RCP, MSET = _mk_helpers(p)

    din = lambda n, s, d=F32: nc.dram_tensor(n, s, d, kind="ExternalInput").ap()
    dout = lambda n, s, d=F32: nc.dram_tensor(n, s, d, kind="ExternalOutput").ap()

    x0 = din("x0", [T, D])
    Wq, Wk, Wv, Wo = (din(n, [D, D]) for n in ("Wq", "Wk", "Wv", "Wo"))
    Wdkv = din("Wdkv", [D, DL]); Wupk = din("Wupk", [DL, D]); Wupv = din("Wupv", [DL, D])
    WgT = din("WgT", [D, E0])
    W1T = din("W1T", [E0, D, I0]); W3T = din("W3T", [E0, D, I0]); W2T = din("W2T", [E0, I0, D])
    b13c = din("b13c", [E0, 128, 16]); b2 = din("b2", [E0, D])
    A_log = din("A_log", [D, 16])
    WxdtT = din("WxdtT", [D, 64]); b_xdt = din("b_xdt", [64])
    WdtT = din("WdtT", [64, D]); b_dt = din("b_dt", [D])
    WBT = din("WBT", [D, 16]); b_B = din("b_B", [16])
    WCT = din("WCT", [D, 16]); b_C = din("b_C", [16])
    Dvec = din("Dvec", [D])
    gwT = din("gwT", [D, E1])
    ident_in = din("identc", [128, 128])
    Erow = din("Erow", [2, T])
    Cband_in = din("CbandC", [128, 896])
    xn_spill = nc.dram_tensor("xn_spill", [128, NS * T], BF16, kind="Internal").ap()
    sums_spill = nc.dram_tensor("sums_spill", [16, T], F32, kind="Internal").ap()
    bc_spill = nc.dram_tensor("bc_spill", [16, 2 * T], BF16, kind="Internal").ap()
    bc_rep = nc.dram_tensor("bc_rep", [128, 16 * 2 * T], BF16, kind="Internal").ap()

    partial = dout("partial", [T, D])
    xn1f = dout("xn1f", [128, NS * T], BF16)
    R_out = dout("R_out", [T, E1])
    p_out = dout("p_out", [T, E1])
    dbg = {}

    def dbg_out(name, shape, dt=F32):
        dbg[name] = dout("dbg_" + name, shape, dt)
        return dbg[name]

    sb = lambda n, s, d: stk.enter_context(nc.sbuf_tensor(n, s, d))
    T1 = sb("T1", [128, 2 * NS * T], BF16)
    T1f = T1.ap().bitcast(F32)
    T2 = sb("T2", [128, NT * H * 65], BF16)
    T2f = T2.ap().bitcast(F32)
    T3 = sb("T3", [128, NS * T], BF16)
    T5 = sb("T5", [128, 4096], BF16)
    TQ = sb("TQ", [128, 4096], BF16)
    TK = sb("TK", [128, 4096], BF16)
    T7 = sb("T7", [128, 1024], BF16)
    T8 = sb("T8", [128, 896], F32)
    T9 = sb("T9", [128, 2048], F32)
    T10 = sb("T10", [128, 4096], BF16)
    T11 = sb("T11", [128, 2048], F32)
    T12 = sb("T12", [128, 2048], F32)
    T12b = T12.ap().bitcast(BF16)
    identb = sb("identb", [128, 128], BF16)
    identf = sb("identf", [128, 128], F32)
    CC = sb("CCt", [128, 16], F32)

    PA = stk.enter_context(nc.psum_tensor("PA", [128, 2048], F32))
    PB = stk.enter_context(nc.psum_tensor("PB", [128, 1024], F32))
    PC = stk.enter_context(nc.psum_tensor("PC", [128, 512], F32))
    PD = stk.enter_context(nc.psum_tensor("PD", [128, 512], F32))
    PCb = PC.ap().bitcast(BF16)
    PAb = PA.ap().bitcast(BF16)
    def pa_bank_bf(c):
        return PAb[:, (c % 4) * 1024:(c % 4) * 1024 + 128]

    ci = p.dma("sync", identf.ap(), ident_in)
    CZ = dict(neg8=0, eps=1, one=2, negone=3, isq2=4, zero=5)
    for k, v in [("neg8", -8.0), ("eps", 1e-5), ("one", 1.0), ("negone", -1.0),
                 ("isq2", 1.0 / math.sqrt(2.0)), ("zero", 0.0)]:
        MSET(CC.ap()[:, CZ[k]:CZ[k] + 1], v)

    def cc(name, n=128, base=0):
        return CC.ap()[base:base + n, CZ[name]:CZ[name] + 1]

    CB_done = p.dma("sync", T8.ap(), Cband_in)
    idb = CP(identb.ap(), identf.ap(), wait=[("dq_sync", ci)])

    if upto == "c0":
        return _finishA(nc, p, stk, dbg)
    xn_stage = TQ.ap()[:, 0:2048]
    for j in range(NT):
        slot = j % 2
        st = T9.ap()[:, slot * 1024:(slot + 1) * 1024]
        cx = p.dma("sync", st, x0[j * 128:(j + 1) * 128, :],
                   wait=p.need(("t9", slot)))
        wx = [("dq_sync", cx)]
        sc = T11.ap()[:, slot * 8: slot * 8 + 8]
        s1, s2, mean, var, rstd, nb, sd = (sc[:, i:i + 1] for i in range(7))
        junk = T12.ap()[:, slot * 1024:(slot + 1) * 1024]
        a1 = ACTF(junk, st, AF.Square, accum=s2, wait=wx + p.need(("lnsc", slot)))
        RED(s1, st, ALU.add, wait=wx)
        TS(mean, s1, 1.0 / D)
        TS(var, s2, 1.0 / D, wait=[("act", a1)])
        m2 = sc[:, 7:8]
        TT(m2, mean, mean, ALU.mult)
        TT(var, var, m2, ALU.subtract)
        a2 = ACTF(sd, var, AF.Sqrt, bias=cc("eps"), wait=[("dve", p.now("dve"))])
        RCP(rstd, sd, wait=[("act", a2)])
        dnb = STT(nb, mean, -1.0, rstd, ALU.mult, ALU.mult)
        if upto == "a1a":
            return _finishA(nc, p, stk, dbg)
        xst = xn_stage[:, slot * 1024:(slot + 1) * 1024]
        d1 = TS(xst, st, rstd, nb, ALU.mult, ALU.add, wait=p.need(("xst", slot)))
        p.used(("t9", slot), "dve", d1)
        p.used(("lnsc", slot), "dve", d1)
        if upto == "a1b":
            return _finishA(nc, p, stk, dbg)
        for c in range(NS):
            pt = pa_bank_bf(c)
            tr = TRN(pt, xst[:, c * 128:(c + 1) * 128], identb.ap(),
                     wait=[("dve", d1), ("dve", idb)] + p.need(("pc", c % 4)))
            ev = CP(T3.ap()[:, c * T + j * 128: c * T + (j + 1) * 128], pt,
                    wait=[("pe", tr)])
            p.used(("pc", c % 4), "dve", ev)
            if upto == "a1c":
                return _finishA(nc, p, stk, dbg)
        p.used(("xst", slot), "pe", p.now("pe"))
    ln_done = p.now("dve")
    if upto == "a1":
        return _finishA(nc, p, stk, dbg)
    sp_done = p.dma("gpsimd", xn_spill, T3.ap(), wait=[("dve", ln_done)])
    if "xn_fm" in debug:
        p.dma("gpsimd", dbg_out("xn_fm", [128, NS * T], BF16), T3.ap(),
              wait=[("dve", ln_done)])
    if upto == "a2":
        return _finishA(nc, p, stk, dbg)

    QB, KB = 0, NS * T

    def w_guard(key, val):
        p.used(key, "pe", val)

    def proj_into(Wsrc, dest):
        for m in range(NS):
            slot = m % 2
            st = T9.ap()[:, slot * 1024:(slot + 1) * 1024]
            wbf = T10.ap()[:, slot * 1024:(slot + 1) * 1024]
            cd = p.dma("sync", st.rearrange("p (c n) -> p c n", n=128),
                       Wsrc[:, m * 128:(m + 1) * 128].rearrange("(c p) n -> p c n", p=128),
                       wait=p.need(("t9", slot)))
            cast = CP(wbf, st, wait=[("dq_sync", cd)] + p.need(("t10", slot)))
            p.used(("t9", slot), "dve", cast)
            last = None
            for jc in range(NJ):
                ps = PA.ap()[:, jc * 512:(jc + 1) * 512]
                w0 = [("dve", cast), ("dve", ln_done)] + p.need(("pa", jc))
                for kc in range(NS):
                    last = MM(ps, wbf[:, kc * 128:(kc + 1) * 128],
                              T3.ap()[:, kc * T + jc * 512: kc * T + (jc + 1) * 512],
                              start=(kc == 0), stop=(kc == NS - 1),
                              wait=(w0 if kc == 0 else ()))
                ev = CP(dest(m, jc), ps, wait=[("pe", last)])
                p.used(("pa", jc), "dve", ev)
            w_guard(("t10", slot), p.now("pe"))

    proj_into(Wq, lambda m, jc: T1.ap()[:, QB + m * T + jc * 512: QB + m * T + (jc + 1) * 512])
    KV = T2.ap()[:, 0:NS * T]
    proj_into(Wk, lambda m, jc: KV[:, m * T + jc * 512: m * T + (jc + 1) * 512])
    kv_done = p.now("dve")

    wdkv_bf = T10.ap()[:, 2048:2048 + 8 * DL]
    st = T9.ap()[:, 0:1024]
    cd = p.dma("sync", st.rearrange("p (c n) -> p c n", n=DL),
               Wdkv.rearrange("(c p) n -> p c n", p=128)[:, 0:4, :],
               wait=p.need(("t9", 0)))
    c1 = CP(wdkv_bf[:, 0:4 * DL], st, wait=[("dq_sync", cd)])
    st2 = T9.ap()[:, 1024:2048]
    cd2 = p.dma("sync", st2.rearrange("p (c n) -> p c n", n=DL),
                Wdkv.rearrange("(c p) n -> p c n", p=128)[:, 4:8, :],
                wait=p.need(("t9", 1)))
    c2 = CP(wdkv_bf[:, 4 * DL:8 * DL], st2, wait=[("dq_sync", cd2)])
    p.used(("t9", 0), "dve", c1)
    p.used(("t9", 1), "dve", c2)
    wd_done = p.now("dve")

    def latent_pass(first):
        for ls in range(2):
            for jc in range(NJ):
                ps = PB.ap()[:, (jc % 2) * 512:((jc % 2) + 1) * 512]
                w0 = [("dve", wd_done), ("dve", p.now("dve"))] + p.need(("pb", jc % 2))
                for kc in range(NS):
                    last = MM(ps, wdkv_bf[:, kc * DL + ls * 128: kc * DL + (ls + 1) * 128],
                              KV[:, kc * T + jc * 512: kc * T + (jc + 1) * 512],
                              start=(kc == 0), stop=(kc == NS - 1),
                              wait=(w0 if kc == 0 else ()))
                sl = T5.ap()[:, ls * T + jc * 512: ls * T + (jc + 1) * 512]
                if first:
                    ev = CP(sl, ps, wait=[("pe", last)])
                else:
                    ev = TT(sl, sl, ps, ALU.add, wait=[("pe", last)])
                p.used(("pb", jc % 2), "dve", ev)

    latent_pass(True)
    lat_k = p.now("dve")
    proj_into(Wv, lambda m, jc: KV[:, m * T + jc * 512: m * T + (jc + 1) * 512])
    latent_pass(False)
    lat_done = p.now("dve")

    wup_bf = T10.ap()[:, 0:2048]
    for kc in range(2):
        slot = kc % 2
        st = T9.ap()[:, slot * 1024:(slot + 1) * 1024]
        cd = p.dma("sync", st, Wupk[kc * 128:(kc + 1) * 128, :], wait=p.need(("t9", slot)))
        cw = CP(wup_bf[:, kc * 1024:(kc + 1) * 1024], st,
                wait=[("dq_sync", cd)] + p.need(("t10", slot)))
        p.used(("t9", slot), "dve", cw)
    wupk_done = p.now("dve")
    for m in range(NS):
        for jc in range(NJ):
            ps = PA.ap()[:, jc * 512:(jc + 1) * 512]
            w0 = [("dve", wupk_done), ("dve", lat_done)] + p.need(("pa", jc))
            for kc in range(2):
                last = MM(ps, wup_bf[:, kc * 1024 + m * 128: kc * 1024 + (m + 1) * 128],
                          T5.ap()[:, kc * T + jc * 512: kc * T + (jc + 1) * 512],
                          start=(kc == 0), stop=(kc == 1), wait=(w0 if kc == 0 else ()))
            ev = CP(T1.ap()[:, KB + m * T + jc * 512: KB + m * T + (jc + 1) * 512], ps,
                    wait=[("pe", last)])
            p.used(("pa", jc), "dve", ev)
    krec_done = p.now("dve")

    wupv_bf = T10.ap()[:, 2048 + 8 * DL: 2048 + 8 * DL + 0]
    wupv_bf = T10.ap()[:, 0:2048]
    vh = T2.ap().rearrange("p (i h c) -> p i h c", i=NT, h=H)
    ones_done = MSET(vh[:, :, :, 64:65], 1.0, wait=[("dve", lat_done),
                                                    ("pe", p.now("pe"))])
    for kc in range(2):
        slot = kc % 2
        st = T9.ap()[:, slot * 1024:(slot + 1) * 1024]
        cd = p.dma("sync", st, Wupv[kc * 128:(kc + 1) * 128, :], wait=p.need(("t9", slot)))
        cw = CP(wupv_bf[:, kc * 1024:(kc + 1) * 1024], st,
                wait=[("dq_sync", cd), ("pe", p.now("pe"))])
        p.used(("t9", slot), "dve", cw)
    wupv_done = p.now("dve")
    for i in range(NT):
        ps = PB.ap()
        w0 = [("dve", wupv_done), ("dve", ones_done)] + p.need(("pb", 0)) + p.need(("pb", 1))
        for kc in range(2):
            for n in range(2):
                last = MM(ps[:, n * 512:(n + 1) * 512],
                          T5.ap()[:, kc * T + i * 128: kc * T + (i + 1) * 128],
                          wupv_bf[:, kc * 1024 + n * 512: kc * 1024 + (n + 1) * 512],
                          start=(kc == 0), stop=(kc == 1),
                          wait=(w0 if (kc == 0 and n == 0) else ()))
        ev = CP(vh[:, i, :, 0:64], ps.rearrange("p (h c) -> p h c", h=H),
                wait=[("pe", last)])
        p.used(("pb", 0), "dve", ev)
        p.used(("pb", 1), "dve", ev)
    vhat_done = p.now("dve")

    e1 = p.dma("sync", T9.ap()[:, 0:T], Erow[0:1, :].broadcast_to([128, T]),
               wait=[("dve", vhat_done), ("dve", krec_done)] +
               p.need(("t9", 0)) + p.need(("t9", 1)))
    ce1 = CP(T5.ap()[:, 0:T], T9.ap()[:, 0:T], wait=[("dq_sync", e1)])
    e2 = p.dma("sync", T9.ap()[:, 0:T], Erow[1:2, :].broadcast_to([128, T]),
               wait=[("dve", ce1)])
    ce2 = CP(T5.ap()[:, T:2 * T], T9.ap()[:, 0:T], wait=[("dq_sync", e2)])
    p.used(("t9", 0), "dve", ce2)
    p.used(("t9", 1), "dve", ce2)
    erep_done = p.now("dve")

    if "qkrec" in debug:
        p.dma("gpsimd", dbg_out("q_fm", [128, NS * T], BF16), T1.ap()[:, QB:QB + NS * T],
              wait=[("dve", krec_done)])
        p.dma("gpsimd", dbg_out("krec_fm", [128, NS * T], BF16),
              T1.ap()[:, KB:KB + NS * T], wait=[("dve", krec_done)])
        p.dma("gpsimd", dbg_out("vhat", [128, NT * H * 65], BF16),
              T2.ap()[:, 0:NT * H * 65], wait=[("dve", vhat_done)])
        p.dma("gpsimd", dbg_out("latent", [128, 2 * T], BF16), T5.ap()[:, 0:2 * T],
              wait=[("dve", lat_done)])
    if upto == "a3":
        return _finishA(nc, p, stk, dbg)

    p.barrier()
    EP = lambda rb: T5.ap()[rb:rb + 64, 0:T]
    EM = lambda rb: T5.ap()[rb:rb + 64, T:2 * T]
    for h in range(H):
        sl, rb = h // 2, (h % 2) * 64
        q_src = T1.ap()[rb:rb + 64, QB + sl * T: QB + (sl + 1) * T]
        k_src = T1.ap()[rb:rb + 64, KB + sl * T: KB + (sl + 1) * T]
        qA = TQ.ap()[rb:rb + 64, 0:T]
        qB = TQ.ap()[rb:rb + 64, T:2 * T]
        kA = TK.ap()[rb:rb + 64, 0:T]
        kB = TK.ap()[rb:rb + 64, T:2 * T]
        wq = p.need(("qcopy", h % 2)) + [("dve", erep_done)]
        TT(qA, q_src, EP(rb), ALU.mult, wait=wq)
        TT(qB, q_src, EM(rb), ALU.mult)
        TT(kA, k_src, EM(rb), ALU.mult)
        cpq = TT(kB, k_src, EP(rb), ALU.mult)
        T9b16 = T9.ap().bitcast(BF16)
        exslot = [T7.ap()[:, 0:512], T7.ap()[:, 512:1024],
                  T9b16[:, 2048:2560], T9b16[:, 2560:3072]]
        sc_ps = lambda j: PA.ap()[:, j * 512:(j + 1) * 512]
        pvs = [PB.ap()[0:65, 0:512], PB.ap()[0:65, 512:1024],
               PC.ap()[0:65, :], PD.ap()[0:65, :]]
        for i in range(NT):
            jstr = i // 4
            for j in range(NJ):
                r = i - 4 * j
                kx, qx = (kA, qA) if r >= 0 else (kB, qB)
                mmw = [("dve", cpq)] + p.need(("ps", j))
                mq = MM(sc_ps(j), kx[:, i * 128:(i + 1) * 128],
                        qx[:, j * 512:(j + 1) * 512],
                        start=True, stop=True, wait=mmw)
                p.used(("mq", j), "pe", mq)
            mqlast = p.guard[("mq", 3)][1]
            sc = T9.ap()[:, 0:512]
            o_r = 384 - 128 * (i - 4 * jstr)
            dv = TT(sc, sc_ps(jstr), T8.ap()[:, o_r:o_r + 512], ALU.mult,
                    wait=[("pe", mqlast), ("dq_sync", CB_done)] + p.need(("t9s",)))
            aes = {}
            opair = (jstr // 2) ^ 1
            j0, j1 = 2 * opair, 2 * opair + 1
            exp_pair = (T7.ap()[:, 0:1024] if opair == 0 else T9b16[:, 2048:3072])
            wexp = p.need(("ex", j0)) + p.need(("ex", j1))
            aep = ACTF(exp_pair, PA.ap()[:, j0 * 512:(j1 + 1) * 512], AF.Exp,
                       bias=cc("neg8"), wait=[("pe", mqlast)] + wexp)
            p.used(("ps", j0), "act", aep)
            p.used(("ps", j1), "act", aep)
            aes[j0] = aes[j1] = aep
            for j in (2 * (jstr // 2), 2 * (jstr // 2) + 1):
                wex = p.need(("ex", j))
                if j == jstr:
                    ae = ACTF(exslot[j], sc, AF.Exp, bias=cc("neg8"),
                              wait=[("dve", dv)] + wex)
                    p.used(("t9s",), "act", ae)
                    p.used(("ps", j), "dve", dv)
                else:
                    ae = ACTF(exslot[j], sc_ps(j), AF.Exp, bias=cc("neg8"),
                              wait=[("pe", mqlast)] + wex)
                    p.used(("ps", j), "act", ae)
                aes[j] = ae
            for j in range(NJ):
                mpv = MM(pvs[j], T2.ap()[:, i * H * 65 + h * 65: i * H * 65 + (h + 1) * 65],
                         exslot[j], start=(i == 0), stop=(i == NT - 1),
                         wait=[("act", aes[j])] + (p.need(("pbv", j)) if i == 0 else []))
                p.used(("ex", j), "pe", mpv)
        for j in range(NJ):
            ev1 = CP(T3.ap()[rb:rb + 64, sl * T + j * 512: sl * T + (j + 1) * 512],
                     pvs[j][0:64, :],
                     wait=[("pe", p.now("pe")), ("dq_gpsimd", sp_done)])
            stg = j % 2
            ev2 = CP(T12.ap()[64:65, 1024 + stg * 512: 1024 + (stg + 1) * 512],
                     pvs[j][64:65, :],
                     wait=p.need(("sstage", stg)))
            p.used(("pbv", j), "dve", ev2)
            cdd = p.dma("gpsimd", T12.ap()[h:h + 1, j * 512:(j + 1) * 512],
                        T12.ap()[64:65, 1024 + stg * 512: 1024 + (stg + 1) * 512],
                        wait=[("dve", ev2)])
            p.used(("sstage", stg), "dq_gpsimd", cdd)
        p.used(("qcopy", h % 2), "pe", p.now("pe"))
    sums_done = p.now("dq_gpsimd")
    p.used(("t9w", 0), "act", p.now("act"))
    p.used(("t9w", 1), "act", p.now("act"))
    rc = RCP(T11.ap()[0:16, 0:T], T12.ap()[0:16, 0:T],
             wait=[("dq_gpsimd", sums_done), ("dve", p.now("dve"))])
    rsp = p.dma("gpsimd", sums_spill, T11.ap()[0:16, 0:T], wait=[("dve", rc)])
    for s in range(NS):
        rep = T9.ap()
        cdr = p.dma("gpsimd", rep[0:64, :], sums_spill[2 * s:2 * s + 1, :].broadcast_to([64, T]),
                    wait=p.need(("t9rep",)))
        cdr2 = p.dma("gpsimd", rep[64:128, :],
                     sums_spill[2 * s + 1:2 * s + 2, :].broadcast_to([64, T]))
        nm = TT(T3.ap()[:, s * T:(s + 1) * T], T3.ap()[:, s * T:(s + 1) * T], rep,
                ALU.mult, wait=[("dq_gpsimd", cdr2)])
        p.used(("t9rep",), "dve", nm)
    attn_done = p.now("dve")

    if "attn" in debug:
        p.dma("gpsimd", dbg_out("attn_fm", [128, NS * T], BF16), T3.ap(),
              wait=[("dve", attn_done)])
        p.dma("gpsimd", dbg_out("sums", [128, T]), T12.ap()[:, 0:T],
              wait=[("dve", attn_done)])
    if upto == "a4":
        return _finishA(nc, p, stk, dbg)

    p.barrier()
    for j in range(NT):
        p.dma("sync", T1f[:, j * 1024:(j + 1) * 1024], x0[j * 128:(j + 1) * 128, :])
    x0_done = p.now("dq_sync")
    for half in range(2):
        wo_bf = T10.ap()[:, 0:4096]
        cast_done = []
        for kc in range(NS):
            slot = kc % 2
            st = T9.ap()[:, slot * 1024: slot * 1024 + 512]
            cd = p.dma("sync", st, Wo[kc * 128:(kc + 1) * 128, half * 512:(half + 1) * 512],
                       wait=p.need(("t9w", slot)))
            cw = CP(wo_bf[:, kc * 512:(kc + 1) * 512], st,
                    wait=[("dq_sync", cd)] + ([("pe", p.now("pe"))] if half else []))
            p.used(("t9w", slot), "dve", cw)
        wo_done = p.now("dve")
        for j in range(NT):
            ps = PC.ap() if j % 2 == 0 else PD.ap()
            w0 = [("dve", wo_done), ("dve", attn_done)] + p.need(("psw", j % 2))
            for kc in range(NS):
                last = MM(ps, T3.ap()[:, kc * T + j * 128: kc * T + (j + 1) * 128],
                          wo_bf[:, kc * 512:(kc + 1) * 512],
                          start=(kc == 0), stop=(kc == NS - 1),
                          wait=(w0 if kc == 0 else ()))
            dst = T1f[:, j * 1024 + half * 512: j * 1024 + (half + 1) * 512]
            ad = TT(dst, dst, ps, ALU.add, wait=[("pe", last), ("dq_sync", x0_done)])
            p.used(("psw", j % 2), "dve", ad)
    x1_attn = p.now("dve")
    if "x1a" in debug:
        p.dma("gpsimd", dbg_out("x1a", [128, NT * 1024]), T1f[:, 0:NT * 1024],
              wait=[("dve", x1_attn)])
    if upto == "a4b":
        return _finishA(nc, p, stk, dbg)

    p.barrier()
    xn_rl = p.dma("sync", T3.ap(), xn_spill)
    wg_bf = T10.ap()[:, 0:NS * E0]
    st = T9.ap()[:, 0:NS * E0]
    cd = p.dma("sync", st.rearrange("p (c n) -> p c n", n=E0),
               WgT.rearrange("(c p) n -> p c n", p=128))
    cwg = CP(wg_bf, st, wait=[("dq_sync", cd)])
    p.used(("mt9a",), "dve", cwg)
    we_tm = T11.ap()[:, 0:NT * E0]
    gsc = T11.ap()[:, 300:396]
    for j in range(NT):
        ps = PD.ap()[:, 0:E0]
        w0 = [("dve", cwg), ("dq_sync", xn_rl)] + p.need(("pdg",))
        for kc in range(NS):
            last = MM(ps, T3.ap()[:, kc * T + j * 128: kc * T + (j + 1) * 128],
                      wg_bf[:, kc * E0:(kc + 1) * E0],
                      start=(kc == 0), stop=(kc == NS - 1), wait=(w0 if kc == 0 else ()))
        sg = T12.ap()[:, 1024:1024 + 16]
        sgw = p.need(("gsc",))
        a1 = ACTF(sg, ps, AF.Sigmoid, wait=[("pe", last)] + sgw)
        p.used(("pdg",), "act", a1)
        gmax = gsc[:, 0:4]; m1 = gsc[:, 4:5]; gex = gsc[:, 5:9]; m2 = gsc[:, 9:10]
        gsel = gsc[:, 10:14]; notg = gsc[:, 14:18]; sme = gsc[:, 18:34]
        sv = gsc[:, 34:50]; m4 = gsc[:, 50:51]; wr = gsc[:, 51:52]
        wi = gsc[:, 52:53]; ws = gsc[:, 53:54]
        keep = gsc[:, 54:70]; msk = gsc[:, 70:86]; mi = gsc[:, 86:87]
        RED(gmax, sg.rearrange("p (g e) -> p g e", g=4), ALU.max, wait=[("act", a1)])
        RED(m1, gmax, ALU.max)
        TS(gex, gmax, m1, None, ALU.is_ge)
        STT(gex, gex, -1e9, gmax, ALU.mult, ALU.add)
        RED(m2, gex, ALU.max)
        TS(gsel, gmax, m2, None, ALU.is_ge)
        TS(notg, gsel, -1.0, 1.0, ALU.mult, ALU.add)
        ng_b = notg.unsqueeze(2).broadcast_to([128, 4, 4])
        STT(sme.rearrange("p (a b) -> p a b", b=4), ng_b, -1e9,
            sg.rearrange("p (a b) -> p a b", b=4), ALU.mult, ALU.add)
        CP(sv, sme)
        for it in range(3):
            RED(mi, sv, ALU.max)
            TS(msk, sv, mi, None, ALU.is_ge)
            STT(sv, msk, -1e9, sv, ALU.mult, ALU.add)
        RED(m4, sv, ALU.max)
        TS(keep, sme, m4, None, ALU.is_ge)
        wvals = T12.ap()[:, 1040:1056]
        TT(wvals, sg, keep, ALU.mult)
        RED(wr, wvals, ALU.add)
        RCP(wi, wr)
        TS(ws, wi, 2.5)
        gd = TS(we_tm[:, j * E0:(j + 1) * E0], wvals, ws, None, ALU.mult)
        p.used(("gsc",), "dve", gd)
        pt = PC.ap()[0:16, 128:256]
        tr = TRN(pt, we_tm[:, j * E0:(j + 1) * E0], identf.ap(),
                 wait=[("dve", gd)] + p.need(("pdt",)))
        evt = CP(TK.ap()[0:16, j * 128:(j + 1) * 128], pt, wait=[("pe", tr)])
        p.used(("pdt",), "dve", evt)
    gate_done = p.now("dve")
    if "gate" in debug:
        p.dma("gpsimd", dbg_out("we_tm", [128, NT * E0]), we_tm, wait=[("dve", gate_done)])
    if upto == "a5g":
        return _finishA(nc, p, stk, dbg)

    b2_bf = TK.ap()[0:16, 2048:2048 + D]
    st = T9.ap()[0:16, 0:D]
    cdb2 = p.dma("sync", st, b2, wait=p.need(("mt9a",)))
    cb2 = CP(b2_bf, st, wait=[("dq_sync", cdb2)])
    p.used(("mt9a",), "dve", cb2)
    for j in range(NT):
        ps = PB.ap()
        w0 = [("dve", cb2), ("dve", gate_done)] + p.need(("pbb",))
        m1 = MM(ps[:, 0:512], TK.ap()[0:16, j * 128:(j + 1) * 128],
                b2_bf[0:16, 0:512], start=True, stop=True, wait=w0)
        m2 = MM(ps[:, 512:1024], TK.ap()[0:16, j * 128:(j + 1) * 128],
                b2_bf[0:16, 512:1024], start=True, stop=True)
        dst = T1f[:, j * 1024:(j + 1) * 1024]
        ad = TT(dst, dst, ps, ALU.add, wait=[("pe", m2)])
        p.used(("pbb",), "dve", ad)
    b2fold_pe = p.now("pe")
    p.used(("mw2", 4), "pe", b2fold_pe)
    p.used(("mw2", 5), "pe", b2fold_pe)
    p.used(("mw2", 6), "pe", b2fold_pe)
    p.used(("mw2", 7), "pe", b2fold_pe)
    for e in range(E0):
        bslot = e % 2
        bc = T11.ap()[:, 400 + bslot * 16: 400 + bslot * 16 + 16]
        cdb3 = p.dma("sync", bc, b13c[e], wait=p.need(("bc", bslot)))
        for isl in range(NS):
            wslot = isl % 2
            s1sl = T9.ap()[:, 0:1024]
            s3sl = T9.ap()[:, 1024:2048]
            w1c = T10.ap()[:, wslot * 1024:(wslot + 1) * 1024]
            w3c = T10.ap()[:, 2048 + wslot * 1024: 2048 + (wslot + 1) * 1024]
            cd1 = p.dma("sync", s1sl.rearrange("p (c n) -> p c n", n=128),
                        W1T[e][:, isl * 128:(isl + 1) * 128].rearrange("(c p) n -> p c n", p=128),
                        wait=p.need(("mt9a",)))
            cw1 = CP(w1c, s1sl, wait=[("dq_sync", cd1)] + p.need(("mw1", wslot)))
            p.used(("mt9a",), "dve", cw1)
            cd3 = p.dma("sync", s3sl.rearrange("p (c n) -> p c n", n=128),
                        W3T[e][:, isl * 128:(isl + 1) * 128].rearrange("(c p) n -> p c n", p=128),
                        wait=p.need(("mt9b",)))
            cw3 = CP(w3c, s3sl, wait=[("dq_sync", cd3)] + p.need(("mw3", wslot)))
            p.used(("mt9b",), "dve", cw3)
            for jc in range(NJ):
                ph1 = PC.ap()
                ph3 = PD.ap()
                w0 = [("dve", cw1), ("dq_sync", cdb3), ("dq_sync", xn_rl)] + \
                    p.need(("ph1",))
                w3w = [("dve", cw3)] + p.need(("ph3",))
                for kc in range(NS):
                    l1 = MM(ph1, w1c[:, kc * 128:(kc + 1) * 128],
                            T3.ap()[:, kc * T + jc * 512: kc * T + (jc + 1) * 512],
                            start=(kc == 0), stop=(kc == NS - 1),
                            wait=(w0 if kc == 0 else ()))
                for kc in range(NS):
                    l3 = MM(ph3, w3c[:, kc * 128:(kc + 1) * 128],
                            T3.ap()[:, kc * T + jc * 512: kc * T + (jc + 1) * 512],
                            start=(kc == 0), stop=(kc == NS - 1),
                            wait=(w3w if kc == 0 else ()))
                s1 = T12b[:, 2048 + (jc % 2) * 512: 2048 + (jc % 2 + 1) * 512]
                aw = p.need(("silu", jc % 2)) + [("dve", gate_done)]
                a1 = ACTF(s1, ph1, AF.Silu, bias=bc[:, isl:isl + 1], wait=[("pe", l1)] + aw)
                p.used(("ph1",), "act", a1)
                hdst = T2.ap()[:, isl * T + jc * 512:isl * T + (jc + 1) * 512]
                hd = STT(hdst, ph3, bc[:, 8 + isl:9 + isl], s1, ALU.add, ALU.mult,
                         wait=[("pe", l3), ("act", a1), ("dve", cb2)] + p.need(("hbuf", isl, jc)))
                p.used(("ph3",), "dve", hd)
                p.used(("silu", jc % 2), "dve", hd)
            w_guard(("mw1", wslot), p.now("pe"))
            w_guard(("mw3", wslot), p.now("pe"))
        h_done = p.now("dve")
        h_pe = p.now("pe")
        w2loc = []
        for isl in range(NS):
            wslot = isl % 2
            st1 = T9.ap()[:, wslot * 1024:(wslot + 1) * 1024]
            dstw = (T10.ap() if isl < 4 else TK.ap())[:, (isl % 4) * 1024:
                                                       ((isl % 4) + 1) * 1024]
            w2loc.append(dstw)
            key9 = ("mt9a",) if wslot == 0 else ("mt9b",)
            cd2 = p.dma("sync", st1, W2T[e][isl * 128:(isl + 1) * 128, :],
                        wait=p.need(key9))
            cw2 = CP(dstw, st1,
                     wait=[("dq_sync", cd2), ("pe", h_pe)] + p.need(("mw2", isl)) +
                     ([("dve", cb2)] if (e == 0 and isl < 2) else []))
            p.used(key9, "dve", cw2)
        w2_done = p.now("dve")
        for j in range(NT):
            ps = PA.ap()[:, (j % 2) * 1024:(j % 2 + 1) * 1024]
            w0 = [("dve", w2_done), ("dve", h_done)] + p.need(("pay", j % 2))
            for isl in range(NS):
                for n in range(2):
                    last = MM(ps[:, n * 512:(n + 1) * 512],
                              T2.ap()[:, isl * T + j * 128: isl * T + (j + 1) * 128],
                              w2loc[isl][:, n * 512:(n + 1) * 512],
                              start=(isl == 0), stop=(isl == NS - 1),
                              wait=(w0 if (isl == 0 and n == 0) else ()))
            dst = T1f[:, j * 1024:(j + 1) * 1024]
            ac = STT(dst, ps, we_tm[:, j * E0 + e: j * E0 + e + 1], dst,
                     ALU.mult, ALU.add, wait=[("pe", last)])
            p.used(("pay", j % 2), "dve", ac)
        for isl in range(NS):
            p.used(("mw2", isl), "pe", p.now("pe"))
        p.used(("bc", bslot), "dve", p.now("dve"))
        p.used(("mw1", 0), "pe", p.now("pe"))
        p.used(("mw1", 1), "pe", p.now("pe"))
        p.used(("mw3", 0), "pe", p.now("pe"))
        p.used(("mw3", 1), "pe", p.now("pe"))
        for isl in range(NS):
            for jc in range(NJ):
                p.used(("hbuf", isl, jc), "pe", p.now("pe"))
    x1_done = p.now("dve")
    if "x1" in debug:
        p.dma("gpsimd", dbg_out("x1", [128, NT * 1024]), T1f[:, 0:NT * 1024],
              wait=[("dve", x1_done)])
    if upto == "a5":
        return _finishA(nc, p, stk, dbg)

    p.barrier()
    xn_stage2 = TQ.ap()[:, 0:2048]
    for j in range(NT):
        slot = j % 2
        src = T1f[:, j * 1024:(j + 1) * 1024]
        sc = T11.ap()[:, 600 + slot * 8: 600 + slot * 8 + 8]
        s1, s2, mean, var, rstd, nb, sd = (sc[:, i:i + 1] for i in range(7))
        junk = T12.ap()[:, slot * 1024:(slot + 1) * 1024]
        a1 = ACTF(junk, src, AF.Square, accum=s2, wait=p.need(("lnsc2", slot)))
        RED(s1, src, ALU.add)
        TS(mean, s1, 1.0 / D)
        TS(var, s2, 1.0 / D, wait=[("act", a1)])
        m2 = sc[:, 7:8]
        TT(m2, mean, mean, ALU.mult)
        TT(var, var, m2, ALU.subtract)
        a2 = ACTF(sd, var, AF.Sqrt, bias=cc("eps"), wait=[("dve", p.now("dve"))])
        RCP(rstd, sd, wait=[("act", a2)])
        STT(nb, mean, -1.0, rstd, ALU.mult, ALU.mult)
        xst = xn_stage2[:, slot * 1024:(slot + 1) * 1024]
        d1 = TS(xst, src, rstd, nb, ALU.mult, ALU.add, wait=p.need(("xst2", slot)))
        p.used(("lnsc2", slot), "dve", d1)
        for c in range(NS):
            pt = pa_bank_bf(c)
            tr = TRN(pt, xst[:, c * 128:(c + 1) * 128], identb.ap(),
                     wait=[("dve", d1)] + p.need(("pc2", c % 4)))
            ev = CP(T3.ap()[:, c * T + j * 128: c * T + (j + 1) * 128], pt,
                    wait=[("pe", tr)])
            p.used(("pc2", c % 4), "dve", ev)
        p.used(("xst2", slot), "pe", p.now("pe"))
    ln1_done = p.now("dve")
    p.dma("gpsimd", xn1f, T3.ap(), wait=[("dve", ln1_done)])

    wg1_bf = T10.ap()[:, 2048:2048 + NS * E1]
    st = T9.ap()[:, 0:NS * E1]
    cdg = p.dma("sync", st.rearrange("p (c n) -> p c n", n=E1),
                gwT.rearrange("(c p) n -> p c n", p=128), wait=p.need(("mt9a",)))
    cwg1 = CP(wg1_bf, st, wait=[("dq_sync", cdg)])
    probs_tm = T11.ap()[:, 0:NT * E1]
    R_tm = T11.ap()[:, 128:128 + NT * E1]
    for j in range(NT):
        ps = PD.ap()[:, 0:E1]
        w0 = [("dve", cwg1), ("dve", ln1_done)] + p.need(("pdg2",))
        for kc in range(NS):
            last = MM(ps, T3.ap()[:, kc * T + j * 128: kc * T + (j + 1) * 128],
                      wg1_bf[:, kc * E1:(kc + 1) * E1],
                      start=(kc == 0), stop=(kc == NS - 1), wait=(w0 if kc == 0 else ()))
        g = T11.ap()[:, 400:472]
        mrow = g[:, 0:1]; nm = g[:, 1:2]; ssum = g[:, 2:3]; sinv = g[:, 3:4]
        ex = T12.ap()[:, 1024:1032]
        gw2 = p.need(("gsc2",))
        mx = RED(mrow, ps, ALU.max, wait=[("pe", last)] + gw2)
        TS(nm, mrow, -1.0)
        ae = ACTF(ex, ps, AF.Exp, bias=nm, wait=[("dve", p.now("dve"))] + p.need(("ex2",)))
        p.used(("pdg2",), "act", ae)
        RED(ssum, ex, ALU.add, wait=[("act", ae)])
        RCP(sinv, ssum)
        pj = probs_tm[:, j * E1:(j + 1) * E1]
        TS(pj, ex, sinv, None, ALU.mult)
        p.used(("ex2",), "dve", p.now("dve"))
        pe_ = pj.unsqueeze(2).broadcast_to([128, E1, E1])
        pep = pj.unsqueeze(1).broadcast_to([128, E1, E1])
        gt = T12.ap()[:, 1032:1096]
        contrib = T12.ap()[:, 1096:1160]
        gt3 = gt.rearrange("p (a b) -> p a b", b=E1)
        c3 = contrib.rearrange("p (a b) -> p a b", b=E1)
        TT(gt3, pep, pe_, ALU.is_gt)
        TT(c3, gt3, pep, ALU.mult)
        G = g[:, 4:12]; rnk = g[:, 12:20]; asn = g[:, 20:28]
        t1 = g[:, 28:36]; t2 = g[:, 36:44]; pr = g[:, 44:52]
        RED(G, contrib.rearrange("p (a b) -> p a b", a=E1), ALU.add)
        RED(rnk, gt.rearrange("p (a b) -> p a b", a=E1), ALU.add)
        TS(rnk, rnk, 1.0, None, ALU.add)
        TS(asn, G, 0.9, None, ALU.is_lt)
        TT(pr, pj, rnk, ALU.subtract)
        TT(t1, pr, asn, ALU.mult)
        TS(t2, asn, 1e9, -1e9, ALU.mult, ALU.add)
        gR = TT(R_tm[:, j * E1:(j + 1) * E1], t1, t2, ALU.add)
        p.used(("gsc2",), "dve", gR)
    gate1_done = p.now("dve")
    p.dma("gpsimd", R_out.rearrange("(j p) e -> p j e", p=128),
          R_tm.rearrange("p (j e) -> p j e", e=E1), wait=[("dve", gate1_done)])
    p.dma("gpsimd", p_out.rearrange("(j p) e -> p j e", p=128),
          probs_tm.rearrange("p (j e) -> p j e", e=E1), wait=[("dve", gate1_done)])
    if upto == "a6":
        return _finishA(nc, p, stk, dbg)

    p.barrier()
    T7f = T7.ap().bitcast(F32)
    Acols = T7f[:, 0:128]
    for s in range(NS):
        p.dma("sync", Acols[:, s * 16:(s + 1) * 16], A_log[s * 128:(s + 1) * 128, :])
    aload = p.now("dq_sync")
    ae = ACTF(Acols, Acols, AF.Exp, wait=[("dq_sync", aload)])
    aneg = TS(Acols, Acols, -1.0, wait=[("act", ae)])
    bdt = T7f[:, 128:136]
    for s in range(NS):
        p.dma("sync", bdt[:, s:s + 1], b_dt[s * 128:(s + 1) * 128].unsqueeze(1))
    dcol = T7f[:, 136:144]
    for s in range(NS):
        p.dma("sync", dcol[:, s:s + 1], Dvec[s * 128:(s + 1) * 128].unsqueeze(1))
    wx_bf = T10.ap()[:, 0:NS * 64]
    st = T9.ap()[:, 0:NS * 64]
    cdx = p.dma("sync", st.rearrange("p (c n) -> p c n", n=64),
                WxdtT.rearrange("(c p) n -> p c n", p=128))
    cwx = CP(wx_bf, st, wait=[("dq_sync", cdx)])
    bxc = T7f[:, 144:145]
    cbx = p.dma("sync", bxc[0:64, :], b_xdt.unsqueeze(1))
    for jc in range(NJ):
        ps = PA.ap()[0:64, jc * 512:(jc + 1) * 512]
        w0 = [("dve", cwx), ("dve", ln1_done)]
        for kc in range(NS):
            last = MM(ps, wx_bf[:, kc * 64:(kc + 1) * 64],
                      T3.ap()[:, kc * T + jc * 512: kc * T + (jc + 1) * 512],
                      start=(kc == 0), stop=(kc == NS - 1), wait=(w0 if kc == 0 else ()))
    xdt_bf = TK.ap()[0:64, 0:T]
    cxb = TS(xdt_bf, PA.ap()[0:64, 0:T], bxc[0:64, :], None, ALU.add,
             wait=[("pe", last), ("dq_sync", cbx)])

    wbc_bf = T10.ap()[:, 1024:1024 + NS * 32]
    st = T9.ap()[:, 1024:1024 + NS * 16]
    cdB = p.dma("sync", st.rearrange("p (c n) -> p c n", n=16),
                WBT.rearrange("(c p) n -> p c n", p=128))
    cwB = CP(wbc_bf[:, 0:NS * 16], st, wait=[("dq_sync", cdB)])
    cdC = p.dma("sync", st.rearrange("p (c n) -> p c n", n=16),
                WCT.rearrange("(c p) n -> p c n", p=128), wait=[("dve", cwB)])
    cwC = CP(wbc_bf[:, NS * 16:NS * 32], st, wait=[("dq_sync", cdC)])
    onesr = T10.ap()[0:1, 1536:1664]
    MSET(onesr, 1.0)
    st_bb = T11.ap()[0:1, 1950:1982]
    cbb = p.dma("sync", st_bb[:, 0:16], b_B.unsqueeze(0))
    cbc = p.dma("sync", st_bb[:, 16:32], b_C.unsqueeze(0))
    cvb = CP(T10.ap()[0:1, 1664:1696], st_bb, wait=[("dq_sync", cbc)])
    bBr = T10.ap()[0:1, 1664:1680]
    bCr = T10.ap()[0:1, 1680:1696]
    B_fm = TK.ap()[0:16, 2048:4096]
    C_fm = T10.ap()[0:16, 2048:4096]
    for j in range(NT):
        for which in range(2):
            ps = PD.ap()[:, 0:16]
            w0 = [("dve", cwC), ("dve", cvb), ("dve", ln1_done)] + p.need(("pdbc",))
            for kc in range(NS):
                MM(ps, T3.ap()[:, kc * T + j * 128: kc * T + (j + 1) * 128],
                   wbc_bf[:, which * NS * 16 + kc * 16:which * NS * 16 + (kc + 1) * 16],
                   start=(kc == 0), stop=False, wait=(w0 if kc == 0 else ()))
            last = MM(ps, onesr, (bBr if which == 0 else bCr)[:, 0:16],
                      start=False, stop=True)
            bcs = T11.ap()[:, 1024 + which * 16: 1040 + which * 16]
            ev = CP(bcs, ps, wait=[("pe", last)] + p.need(("bcs", which)))
            p.used(("pdbc",), "dve", ev)
            pt = PC.ap()[0:16, 128:256]
            tr = TRN(pt, bcs, identf.ap(), wait=[("dve", ev)] + p.need(("pdt2",)))
            dstf = (B_fm if which == 0 else C_fm)[:, j * 128:(j + 1) * 128]
            evt = CP(dstf, pt, wait=[("pe", tr)])
            p.used(("pdt2",), "dve", evt)
            p.used(("bcs", which), "pe", tr)
    bc_done = p.now("dve")

    bsp = p.dma("gpsimd", bc_spill[:, 0:T], B_fm, wait=[("dve", bc_done)])
    csp = p.dma("gpsimd", bc_spill[:, T:2 * T], C_fm, wait=[("dve", bc_done)])
    for n in range(16):
        p.dma("gpsimd", bc_rep[:, n * 2 * T:(n + 1) * 2 * T],
              bc_spill[n:n + 1, :].broadcast_to([128, 2 * T]))
    bcrep_done = p.now("dq_gpsimd")
    wdt_bf = T10.ap()[0:64, 0:1024]
    stw = T9.ap()[0:64, 0:1024]
    cdw = p.dma("sync", stw, WdtT, wait=[("dve", p.now("dve"))])
    cwd = CP(wdt_bf, stw, wait=[("dq_sync", cdw)])

    if "pre_scan" in debug:
        p.dma("gpsimd", dbg_out("xdt", [64, T], BF16), xdt_bf, wait=[("dve", cxb)])
        p.dma("gpsimd", dbg_out("B_fm", [16, T], BF16), B_fm, wait=[("dve", bc_done)])
        p.dma("gpsimd", dbg_out("C_fm", [16, T], BF16), C_fm, wait=[("dve", bc_done)])

    p.barrier()
    dt_bf = lambda sl: T2.ap()[:, sl * T:(sl + 1) * T]
    scr1 = T9.ap()
    scr2 = T12.ap()
    TKf = TK.ap().bitcast(F32)
    for sl in range(NS):
        pss = PA.ap()
        w0 = [("dve", cwd), ("dve", cxb), ("pool", p.now("pool")),
              ("dve", p.now("dve")), ("act", p.now("act"))] + p.need(("pa_scan",))
        for jc in range(NJ):
            last = MM(pss[:, jc * 512:(jc + 1) * 512],
                      wdt_bf[:, sl * 128:(sl + 1) * 128],
                      xdt_bf[:, jc * 512:(jc + 1) * 512],
                      start=True, stop=True, wait=(w0 if jc == 0 else ()))
        bcol = bdt[:, sl:sl + 1]
        a1 = ACTF(scr1, pss, AF.Abs, bias=bcol,
                  wait=[("pe", last)] + p.need(("scr1",)))
        a2 = ACTF(scr2, scr1, AF.Exp, scale=cc("negone"),
                  wait=[("act", a1)] + p.need(("scr2",)))
        a3 = ACTF(scr1, scr2, AF.Ln, bias=cc("one"))
        mx = TS(scr2, pss, bcol, cc("zero"), ALU.add, ALU.max,
                wait=[("act", a2)])
        p.used(("pa_scan",), "dve", mx)
        dtd = TT(dt_bf(sl), scr2, scr1, ALU.add,
                 wait=[("act", a3)] + p.need(("dtb", sl)))
        p.used(("scr1",), "dve", dtd)
        p.used(("scr2",), "dve", dtd)
    dt_done = p.now("dve")

    du = TQ.ap()[:, 0:T]
    T9b16 = T9.ap().bitcast(BF16)
    d0buf = [TK.ap()[:, 0:T], TK.ap()[:, T:2 * T]]
    p.used(("d0b", 0), "dq_gpsimd", csp)
    p.used(("d0b", 1), "dq_gpsimd", csp)
    d1buf = [T12b[:, 0:T], T12b[:, T:2 * T]]
    hbuf = [TQ.ap()[:, T:2 * T], T9b16[:, 0:T]]
    yac = T11.ap()
    for sl in range(NS):
        dud = TT(du, dt_bf(sl), T3.ap()[:, sl * T:(sl + 1) * T], ALU.mult,
                 wait=[("dve", dt_done)] + p.need(("dub",)))
        yid = TS(yac, T3.ap()[:, sl * T:(sl + 1) * T], dcol[:, sl:sl + 1], None,
                 ALU.mult, wait=p.need(("yac",)) + [("pool", p.now("pool"))])
        cBC = [None] * 17
        cBC[0] = p.dma("sync", T5.ap(), bc_rep[:, 0:2 * T],
                       wait=p.need(("bcrep",)) + [("dq_gpsimd", bcrep_done)])
        for n in range(16):
            pn = n % 2
            d0 = ACTF(d0buf[pn], dt_bf(sl), AF.Exp, scale=Acols[:, sl * 16 + n:sl * 16 + n + 1],
                      wait=[("dve", aneg), ("dve", dud)] + p.need(("d0b", pn)))
            d1 = TT(d1buf[pn], du, T5.ap()[:, 0:T], ALU.mult,
                    wait=[("dq_sync", cBC[n])] + p.need(("d1b", pn)))
            hsc = SCAN(hbuf[pn], d0buf[pn], d1buf[pn], 0.0,
                       wait=[("act", d0)] + p.need(("hb", pn)))
            p.used(("d0b", pn), "dve", hsc)
            p.used(("d1b", pn), "dve", hsc)
            hc = TT(d1buf[pn], hbuf[pn], T5.ap()[:, T:2 * T], ALU.mult,
                    wait=[("dve", hsc)])
            p.used(("hb", pn), "dve", hc)
            if n < 15:
                cBC[n + 1] = p.dma("sync", T5.ap(), bc_rep[:, (n + 1) * 2 * T:(n + 2) * 2 * T],
                                   wait=[("dve", d1), ("dve", hc)])
            else:
                p.used(("bcrep",), "dve", hc)
            ya = TT(yac, yac, d1buf[pn], ALU.add, wait=[("dve", hc)] + p.need(("yp",)),
                    eng="gpsimd")
            p.used(("yp",), "pool", ya)
            p.used(("d1b", pn), "pool", ya)
        p.wait("vector", [("pool", p.now("pool"))])
        ybf = T12b[:, 0:T]
        cy = CP(ybf, yac, wait=p.need(("d1b", 0)) + p.need(("ybf",)) +
                [("pool", p.now("pool"))])
        p.used(("yac",), "dve", cy)
        p.used(("dub",), "dve", cy)
        p.used(("dtb", sl), "dve", cy)
        p.used(("d1b", 0), "dve", cy)
        for j in range(NT):
            pt = pa_bank_bf(j)
            tr = TRN(pt, ybf[:, j * 128:(j + 1) * 128], identb.ap(),
                     wait=[("dve", cy)] + p.need(("pc3", j % 4)) +
                     p.need(("pa_scan",)))
            dst = T1f[:, j * 1024 + sl * 128: j * 1024 + (sl + 1) * 128]
            ad = TT(dst, dst, pt, ALU.add, wait=[("pe", tr), ("dve", x1_done)])
            p.used(("pc3", j % 4), "dve", ad)
        p.used(("ybf",), "pe", p.now("pe"))
        p.used(("pa_scan",), "dve", p.now("dve"))
    scan_done = p.now("dve")

    p.dma("gpsimd", partial.rearrange("(j p) d -> p j d", p=128),
          T1f.rearrange("p (j d) -> p j d", d=1024), wait=[("dve", scan_done)])
    if "partial" in debug:
        p.dma("gpsimd", dbg_out("partial2", [128, NT * 1024]), T1f[:, 0:NT * 1024],
              wait=[("dve", scan_done)])
    return _finishA(nc, p, stk, dbg)


def _finishA(nc, p, stk, dbg):
    p.barrier()
    p.build(stk)
    return nc, p, stk, dbg


def _np_t(a):
    return np.ascontiguousarray(np.swapaxes(np.asarray(a), -1, -2))


def prep_constants():
    t = np.arange(T, dtype=np.float64)
    s = 1.0 / math.sqrt(math.sqrt(1024.0))
    s = (1.0 / 32.0) ** 0.5
    Erow = np.stack([np.exp(LAM * t) * s, np.exp(-LAM * t) * s]).astype(np.float32)
    pcol = np.arange(128, dtype=np.float64)[:, None]
    x = np.arange(896, dtype=np.float64)[None, :]
    Cband = np.minimum(1.0, np.exp(-2.0 * LAM * (x - 384.0 - pcol))).astype(np.float32)
    ident = np.eye(128, dtype=np.float32)
    return dict(identc=ident, Erow=Erow, CbandC=Cband)


def prep_A(inputs, core):
    g = lambda n: np.asarray(inputs[n])
    m = dict(
        x0=np.ascontiguousarray(g("L0_attn_x")[core]),
        Wq=g("L0_attn_W_q"), Wk=g("L0_attn_W_k"), Wv=g("L0_attn_W_v"),
        Wo=g("L0_attn_W_o"), Wdkv=g("L0_attn_W_down_kv"),
        Wupk=g("L0_attn_W_up_k"), Wupv=g("L0_attn_W_up_v"),
        WgT=_np_t(g("L0_ffn_Wg")),
        W1T=_np_t(g("L0_ffn_W1")), W3T=_np_t(g("L0_ffn_W3")), W2T=_np_t(g("L0_ffn_W2")),
        b13c=np.concatenate([
            g("L0_ffn_b1").reshape(E0, 8, 128).transpose(0, 2, 1),
            g("L0_ffn_b3").reshape(E0, 8, 128).transpose(0, 2, 1)], axis=2),
        b2=g("L0_ffn_b2"),
        A_log=g("L1_attn_A_log"),
        WxdtT=_np_t(g("L1_attn_W_xdt")), b_xdt=g("L1_attn_b_xdt"),
        WdtT=_np_t(g("L1_attn_W_dt")), b_dt=g("L1_attn_b_dt"),
        WBT=_np_t(g("L1_attn_W_B")), b_B=g("L1_attn_b_B"),
        WCT=_np_t(g("L1_attn_W_C")), b_C=g("L1_attn_b_C"),
        Dvec=g("L1_attn_D"),
        gwT=_np_t(g("L1_ffn_gate_w")),
    )
    m.update(prep_constants())
    return {k: np.ascontiguousarray(np.asarray(v, dtype=np.float32)) for k, v in m.items()}


def build_progB(debug=()):
    nc = bass.Bass("TRN2", target_bir_lowering=False, debug=False)
    stk = ExitStack()
    p = Prog(nc)
    MM, TRN, CP, TT, TS, STT, ACTF, RED, SCAN, SEL, RCP, MSET = _mk_helpers(p)

    din = lambda n, s, d=F32: nc.dram_tensor(n, s, d, kind="ExternalInput").ap()
    dout = lambda n, s, d=F32: nc.dram_tensor(n, s, d, kind="ExternalOutput").ap()

    partial = din("partial", [T, D])
    xn1f = din("xn1f", [128, NS * T], BF16)
    R_all = din("R_all", [8 * T, E1])
    R_my = din("R_my", [T, E1])
    p_my = din("p_my", [T, E1])
    w1T = din("w1T", [E1, D, F1])
    w2T = din("w2T", [E1, F1, D])
    out = dout("out_f", [T, D])
    dbg = {}

    def dbg_out(name, shape, dt=F32):
        dbg[name] = dout("dbg_" + name, shape, dt)
        return dbg[name]

    sb = lambda n, s, d: stk.enter_context(nc.sbuf_tensor(n, s, d))
    X1 = sb("X1", [128, NT * 1024], F32)
    T3 = sb("T3b", [128, NS * T], BF16)
    T2 = sb("T2b", [128, NS * T], BF16)
    T9 = sb("T9b", [128, 2048], F32)
    T10 = sb("T10b", [128, 4096], BF16)
    TK = sb("TKb", [128, 4096], BF16)
    T11 = sb("T11b", [128, 2048], F32)
    T12 = sb("T12b2", [128, 2048], F32)
    CC = sb("CCb", [128, 16], F32)

    PA = stk.enter_context(nc.psum_tensor("PAb", [128, 2048], F32))
    PB = stk.enter_context(nc.psum_tensor("PBb", [128, 1024], F32))
    PC = stk.enter_context(nc.psum_tensor("PCb2", [128, 512], F32))
    PD = stk.enter_context(nc.psum_tensor("PDb", [128, 512], F32))

    MSET(CC.ap()[:, 0:1], 1.0 / math.sqrt(2.0))
    isq2 = CC.ap()[:, 0:1]
    ones_col = CC.ap()[:, 1:2]
    MSET(ones_col, 1.0)
    onesr = CC.ap()[0:1, 2:3]
    cx = p.dma("sync", X1.ap().rearrange("p (j d) -> p j d", d=1024),
               partial.rearrange("(j p) d -> p j d", p=128))
    cxn = p.dma("sync", T3.ap(), xn1f)
    R_re = T11.ap()[:, 0:1024]
    crall = p.dma("sync", R_re.rearrange("p (c e) -> p c e", e=E1),
                  R_all.rearrange("(c p) e -> p c e", p=128))
    sel_tm = T11.ap()[:, 1024:1024 + NT * E1]
    Rm = T12.ap()[:, 0:NT * E1]
    pm = T12.ap()[:, 256:256 + NT * E1]
    crm = p.dma("sync", Rm.rearrange("p (j e) -> p j e", e=E1),
                R_my.rearrange("(j p) e -> p j e", p=128))
    cpm = p.dma("sync", pm.rearrange("p (j e) -> p j e", e=E1),
                p_my.rearrange("(j p) e -> p j e", p=128))

    lo = T12.ap()[0:1, 512:520]
    hi = T12.ap()[0:1, 520:528]
    mid = T12.ap()[0:1, 528:536]
    cnt = T12.ap()[0:1, 536:544]
    cgt = T12.ap()[0:1, 544:552]
    ones_row = T12.ap()[0:1, 552:680]
    MSET(lo, -10.0)
    MSET(hi, 2.0)
    MSET(ones_row, 1.0)
    mrep = T12.ap()[:, 768:776]
    ge = T9.ap()[:, 0:1024]
    cntpc = T12.ap()[:, 784:792]
    for it in range(34):
        TT(mid, lo, hi, ALU.add)
        TS(mid, mid, 0.5)
        mm1 = MM(PD.ap()[:, 0:E1], ones_row, mid, start=True, stop=True,
                 wait=p.need(("pdrep",)) + [("dve", p.now("dve"))])
        evr = CP(mrep, PD.ap()[:, 0:E1], wait=[("pe", mm1)])
        p.used(("pdrep",), "dve", evr)
        TT(ge.rearrange("p (c e) -> p c e", e=E1),
           R_re.rearrange("p (c e) -> p c e", e=E1),
           mrep.unsqueeze(1).broadcast_to([128, 128, E1]), ALU.is_ge,
           wait=[("dq_sync", crall)])
        RED(cntpc, ge.rearrange("p (c e) -> p e c", e=E1), ALU.add)
        mm2 = MM(PC.ap()[0:1, 0:E1], ones_col, cntpc, start=True, stop=True,
                 wait=[("dve", p.now("dve"))] + p.need(("pccnt",)))
        cg = TS(cgt, PC.ap()[0:1, 0:E1], 2048.0, None, ALU.is_gt,
                wait=[("pe", mm2)])
        p.used(("pccnt",), "dve", cg)
        dlo = T12.ap()[0:1, 680:688]
        dhi = T12.ap()[0:1, 688:696]
        TT(dlo, mid, lo, ALU.subtract)
        TT(dlo, dlo, cgt, ALU.mult)
        TT(lo, lo, dlo, ALU.add)
        TT(dhi, mid, hi, ALU.subtract)
        ncg = T12.ap()[0:1, 696:704]
        TS(ncg, cgt, -1.0, 1.0, ALU.mult, ALU.add)
        TT(dhi, dhi, ncg, ALU.mult)
        TT(hi, hi, dhi, ALU.add)
    tau_done = p.now("dve")
    p.used(("mt9", 0), "dve", tau_done)
    p.used(("mt9", 1), "dve", tau_done)
    mm3 = MM(PD.ap()[:, 0:E1], ones_row, hi, start=True, stop=True,
             wait=p.need(("pdrep",)) + [("dve", tau_done)])
    trep = T12.ap()[:, 800:808]
    evt = CP(trep, PD.ap()[:, 0:E1], wait=[("pe", mm3)])
    keep = T12.ap()[:, 816:824]
    for j in range(NT):
        TT(keep, Rm[:, j * E1:(j + 1) * E1],
           trep, ALU.is_ge, wait=[("dq_sync", cpm), ("dve", evt)])
        TT(sel_tm[:, j * E1:(j + 1) * E1], pm[:, j * E1:(j + 1) * E1], keep, ALU.mult)
    sel_done = p.now("dve")
    if "sel" in debug:
        p.dma("gpsimd", dbg_out("sel", [128, NT * E1]), sel_tm,
              wait=[("dve", sel_done)])

    for e in range(E1):
        for isl in range(NS):
            wslot = isl % 2
            st1 = T9.ap()[:, wslot * 1024:(wslot + 1) * 1024]
            w1c = T10.ap()[:, wslot * 1024:(wslot + 1) * 1024]
            cd1 = p.dma("sync", st1.rearrange("p (c n) -> p c n", n=128),
                        w1T[e][:, isl * 128:(isl + 1) * 128].rearrange(
                            "(c p) n -> p c n", p=128),
                        wait=p.need(("mt9", wslot)))
            cw1 = CP(w1c, st1, wait=[("dq_sync", cd1)] + p.need(("mw1", wslot)))
            p.used(("mt9", wslot), "dve", cw1)
            for jc in range(NJ):
                ph = PC.ap() if jc % 2 == 0 else PD.ap()
                w0 = [("dve", cw1), ("dq_sync", cxn)] + p.need(("ph", jc % 2))
                for kc in range(NS):
                    l1 = MM(ph, w1c[:, kc * 128:(kc + 1) * 128],
                            T3.ap()[:, kc * T + jc * 512: kc * T + (jc + 1) * 512],
                            start=(kc == 0), stop=(kc == NS - 1),
                            wait=(w0 if kc == 0 else ()))
                e1 = T12.ap()[:, 1024 + (jc % 2) * 512: 1024 + (jc % 2 + 1) * 512]
                a1 = ACTF(e1, ph, AF.Erf, scale=isq2,
                          wait=[("pe", l1)] + p.need(("e1s", jc % 2)))
                tmp = T9.ap()[:, 0:0]
                hdst = T2.ap()[:, isl * T + jc * 512: isl * T + (jc + 1) * 512]
                t1 = STT(hdst, ph, 0.5, e1, ALU.mult, ALU.mult,
                         wait=[("act", a1)] + p.need(("hbuf2", isl, jc)))
                t2 = STT(hdst, ph, 0.5, hdst, ALU.mult, ALU.add)
                p.used(("ph", jc % 2), "dve", t2)
                p.used(("e1s", jc % 2), "dve", t2)
            p.used(("mw1", wslot), "pe", p.now("pe"))
        h_done = p.now("dve")
        w2loc = []
        for isl in range(NS):
            wslot = isl % 2
            st1 = T9.ap()[:, wslot * 1024:(wslot + 1) * 1024]
            dstw = (T10.ap() if isl < 4 else TK.ap())[:, (isl % 4) * 1024:
                                                       ((isl % 4) + 1) * 1024]
            w2loc.append(dstw)
            cd2 = p.dma("sync", st1, w2T[e][isl * 128:(isl + 1) * 128, :],
                        wait=p.need(("mt9", wslot)))
            cw2 = CP(dstw, st1, wait=[("dq_sync", cd2)] + p.need(("mw2", isl)))
            p.used(("mt9", wslot), "dve", cw2)
        w2_done = p.now("dve")
        for j in range(NT):
            ps = PA.ap()[:, (j % 2) * 1024:(j % 2 + 1) * 1024]
            w0 = [("dve", w2_done), ("dve", h_done), ("dve", sel_done),
                  ("dq_sync", cx)] + p.need(("pay2", j % 2))
            for isl in range(NS):
                for n in range(2):
                    last = MM(ps[:, n * 512:(n + 1) * 512],
                              T2.ap()[:, isl * T + j * 128: isl * T + (j + 1) * 128],
                              w2loc[isl][:, n * 512:(n + 1) * 512],
                              start=(isl == 0), stop=(isl == NS - 1),
                              wait=(w0 if (isl == 0 and n == 0) else ()))
            dst = X1.ap()[:, j * 1024:(j + 1) * 1024]
            ac = STT(dst, ps, sel_tm[:, j * E1 + e: j * E1 + e + 1], dst,
                     ALU.mult, ALU.add, wait=[("pe", last)])
            p.used(("pay2", j % 2), "dve", ac)
        for isl in range(NS):
            p.used(("mw2", isl), "pe", p.now("pe"))
            for jc in range(NJ):
                p.used(("hbuf2", isl, jc), "pe", p.now("pe"))
    done = p.now("dve")
    p.dma("gpsimd", out.rearrange("(j p) d -> p j d", p=128),
          X1.ap().rearrange("p (j d) -> p j d", d=1024), wait=[("dve", done)])
    p.barrier()
    p.build(stk)
    return nc, p, stk, dbg


def prep_B(inputs, core, partial, xn1f, R_stack, R_my, p_my):
    g = lambda n: np.asarray(inputs[n])
    m = dict(partial=partial, xn1f=xn1f, R_all=R_stack, R_my=R_my, p_my=p_my,
             w1T=_np_t(g("L1_ffn_w1")), w2T=_np_t(g("L1_ffn_w2")))
    return {k: np.ascontiguousarray(np.asarray(v)) for k, v in m.items()}


def kernel(**inputs):
    ncores = 8
    ncA, pA, stkA, _ = build_progA(upto="all", debug=())
    mapsA = [prep_A(inputs, c) for c in range(ncores)]
    resA = run_bass_kernel_spmd(ncA, mapsA, core_ids=list(range(ncores)))
    stkA.close()
    R_stack = np.concatenate([resA.results[c]["R_out"] for c in range(ncores)], 0)
    ncB, pB, stkB, _ = build_progB()
    mapsB = [prep_B(inputs, c, resA.results[c]["partial"], resA.results[c]["xn1f"],
                    R_stack, resA.results[c]["R_out"], resA.results[c]["p_out"])
             for c in range(ncores)]
    resB = run_bass_kernel_spmd(ncB, mapsB, core_ids=list(range(ncores)))
    stkB.close()
    return np.stack([resB.results[c]["out_f"] for c in range(ncores)]).astype(np.float32)


# version 10 (final)
# speedup vs baseline: 1.0347x; 1.0347x over previous
import math
import numpy as np
import concourse.bass as bass
import concourse.mybir as mybir
from concourse.bass_utils import run_bass_kernel_spmd
from contextlib import ExitStack

F32 = mybir.dt.float32
BF16 = mybir.dt.bfloat16
AF = mybir.ActivationFunctionType
ALU = mybir.AluOpType
AX = mybir.AxisListType

D, T, H, HD, DL = 1024, 2048, 16, 64, 256
LAM = 0.01
NS, NT, NJ = 8, 16, 4
E0, I0, E1, F1 = 16, 1024, 8, 1024
ENGINES = ("sync", "scalar", "vector", "tensor", "gpsimd")


class Prog:
    def __init__(self, nc):
        self.nc = nc
        self.q = {e: [] for e in ENGINES}
        self.cnt = {}
        self.waited = {e: {} for e in ENGINES}
        self.sems = {}
        self.guard = {}

    def _wl(self, eng, waits):
        out = []
        for s, v in waits:
            if not v:
                continue
            if self.waited[eng].get(s, 0) < v:
                self.waited[eng][s] = v
                out.append((s, v))
        return out

    _SELF = {"vector": "dve", "scalar": "act", "gpsimd": "pool"}

    def op(self, eng, fn, wait=(), inc=None, inc_amt=1):
        selfsem = self._SELF.get(eng)
        wait = list(wait)
        if selfsem is not None:
            prev = self.cnt.get(selfsem, 0)
            if prev:
                wait.append((selfsem, prev))
            if inc is None:
                inc, inc_amt = selfsem, 1
        wl = self._wl(eng, wait)
        if inc is not None:
            self.cnt[inc] = self.cnt.get(inc, 0) + inc_amt

        def thunk(h, wl=wl, fn=fn, inc=inc, inc_amt=inc_amt):
            for s, v in wl:
                h.wait_ge(self.sems[s], v)
            r = fn(h)
            if inc is not None:
                r.then_inc(self.sems[inc], inc_amt)
        self.q[eng].append(thunk)
        return self.cnt.get(inc) if inc is not None else None

    def dma(self, eng, out, in_, wait=(), fence=True, **kw):
        sem = "dq_" + eng
        wl = self._wl(eng, wait)
        self.cnt[sem] = self.cnt.get(sem, 0) + 16
        cum = self.cnt[sem]

        def thunk(h, wl=wl, out=out, in_=in_, kw=kw, sem=sem, cum=cum, fence=fence):
            for s, v in wl:
                h.wait_ge(self.sems[s], v)
            h.dma_start(out=out, in_=in_, **kw).then_inc(self.sems[sem], 16)
            if fence:
                h.wait_ge(self.sems[sem], cum)
        self.q[eng].append(thunk)
        if fence:
            self.waited[eng][sem] = max(self.waited[eng].get(sem, 0), cum)
        return cum

    def wait(self, eng, waits):
        wl = self._wl(eng, waits)
        if not wl:
            return

        def thunk(h, wl=wl):
            for s, v in wl:
                h.wait_ge(self.sems[s], v)
        self.q[eng].append(thunk)

    def now(self, sem):
        return self.cnt.get(sem, 0)

    def barrier(self):
        snap = [(s, v) for s, v in self.cnt.items()]
        for e in ENGINES:
            self.wait(e, snap)

    def need(self, key):
        g = self.guard.get(key)
        return [g] if g else []

    def used(self, key, sem, val):
        self.guard[key] = (sem, val)

    def build(self, stack):
        for name in list(self.cnt.keys()):
            self.sems[name] = stack.enter_context(self.nc.semaphore(name))
        block = stack.enter_context(self.nc.Block())
        for eng in ENGINES:
            thunks = self.q[eng]
            if not thunks:
                continue

            def body(h, thunks=thunks):
                for t in thunks:
                    t(h)
            getattr(block, eng)(body)


def _mk_helpers(p):
    def MM(ps, lhsT, rhs, start, stop, wait=()):
        return p.op("tensor", lambda t, ps=ps, l=lhsT, r=rhs, a=start, b=stop:
                    t.matmul(out=ps, lhsT=l, rhs=r, start=a, stop=b),
                    wait=wait, inc=("pe" if stop else None))

    def TRN(ps, src, ident, wait=()):
        return p.op("tensor", lambda t, ps=ps, s=src, i=ident:
                    t.transpose(out=ps, in_=s, identity=i), wait=wait, inc="pe")

    def CP(out, in_, wait=(), eng="vector"):
        sem = {"vector": "dve", "gpsimd": "pool", "scalar": "act"}[eng]
        if eng == "scalar":
            return p.op(eng, lambda h, o=out, i=in_: h.copy(out=o, in_=i),
                        wait=wait, inc=sem)
        return p.op(eng, lambda h, o=out, i=in_: h.tensor_copy(out=o, in_=i),
                    wait=wait, inc=sem)

    def TT(out, a, b, alu, wait=(), eng="vector"):
        sem = {"vector": "dve", "gpsimd": "pool"}[eng]
        return p.op(eng, lambda h, o=out, a=a, b=b, op=alu:
                    h.tensor_tensor(out=o, in0=a, in1=b, op=op), wait=wait, inc=sem)

    def TS(out, in0, s1, s2=None, op0=ALU.mult, op1=ALU.add, wait=(), eng="vector"):
        sem = {"vector": "dve", "gpsimd": "pool"}[eng]
        if s2 is None:
            return p.op(eng, lambda h, o=out, i=in0, s=s1, op=op0:
                        h.tensor_scalar(out=o, in0=i, scalar1=s, scalar2=None, op0=op),
                        wait=wait, inc=sem)
        return p.op(eng, lambda h, o=out, i=in0, a=s1, b=s2, o0=op0, o1=op1:
                    h.tensor_scalar(out=o, in0=i, scalar1=a, scalar2=b, op0=o0, op1=o1),
                    wait=wait, inc=sem)

    def STT(out, in0, sc, in1, op0, op1, wait=(), eng="vector"):
        sem = {"vector": "dve", "gpsimd": "pool"}[eng]
        return p.op(eng, lambda h, o=out, i=in0, s=sc, j=in1, o0=op0, o1=op1:
                    h.scalar_tensor_tensor(out=o, in0=i, scalar=s, in1=j, op0=o0, op1=o1),
                    wait=wait, inc=sem)

    def ACTF(out, in_, func, bias=0.0, scale=1.0, accum=None, wait=()):
        return p.op("scalar", lambda h, o=out, i=in_, f=func, b=bias, s=scale, a=accum:
                    h.activation(out=o, in_=i, func=f, bias=b, scale=s, accum_out=a),
                    wait=wait, inc="act")

    def RED(out, in_, alu, wait=(), negate=None):
        return p.op("vector", lambda h, o=out, i=in_, op=alu, n=negate:
                    h.tensor_reduce(out=o, in_=i, axis=AX.X, op=op, negate=n),
                    wait=wait, inc="dve")

    def SCAN(out, d0, d1, init, wait=()):
        return p.op("vector", lambda h, o=out, a=d0, b=d1, i=init:
                    h.tensor_tensor_scan(out=o, data0=a, data1=b, initial=i,
                                         op0=ALU.mult, op1=ALU.add),
                    wait=wait, inc="dve")

    def SEL(out, mask, tv, fv, wait=()):
        return p.op("vector", lambda h, o=out, m=mask, a=tv, b=fv:
                    h.select(out=o, mask=m, on_true=a, on_false=b), wait=wait, inc="dve")

    def RCP(out, in_, wait=()):
        return p.op("vector", lambda h, o=out, i=in_: h.reciprocal(out=o, in_=i),
                    wait=wait, inc="dve")

    def MSET(ap, val, wait=(), eng="vector"):
        sem = {"vector": "dve", "gpsimd": "pool"}[eng]
        return p.op(eng, lambda h, a=ap, v=val: h.memset(a, v), wait=wait, inc=sem)

    return MM, TRN, CP, TT, TS, STT, ACTF, RED, SCAN, SEL, RCP, MSET


def build_progA(upto="all", debug=()):
    nc = bass.Bass("TRN2", target_bir_lowering=False, debug=False)
    stk = ExitStack()
    p = Prog(nc)
    MM, TRN, CP, TT, TS, STT, ACTF, RED, SCAN, SEL, RCP, MSET = _mk_helpers(p)

    din = lambda n, s, d=F32: nc.dram_tensor(n, s, d, kind="ExternalInput").ap()
    dout = lambda n, s, d=F32: nc.dram_tensor(n, s, d, kind="ExternalOutput").ap()

    x0 = din("x0", [T, D])
    Wq, Wk, Wv, Wo = (din(n, [D, D]) for n in ("Wq", "Wk", "Wv", "Wo"))
    Wdkv = din("Wdkv", [D, DL]); Wupk = din("Wupk", [DL, D]); Wupv = din("Wupv", [DL, D])
    WgT = din("WgT", [D, E0])
    W1T = din("W1T", [E0, D, I0]); W3T = din("W3T", [E0, D, I0]); W2T = din("W2T", [E0, I0, D])
    b13c = din("b13c", [E0, 128, 16]); b2 = din("b2", [E0, D])
    A_log = din("A_log", [D, 16])
    WxdtT = din("WxdtT", [D, 64]); b_xdt = din("b_xdt", [64])
    WdtT = din("WdtT", [64, D]); b_dt = din("b_dt", [D])
    WBT = din("WBT", [D, 16]); b_B = din("b_B", [16])
    WCT = din("WCT", [D, 16]); b_C = din("b_C", [16])
    Dvec = din("Dvec", [D])
    gwT = din("gwT", [D, E1])
    ident_in = din("identc", [128, 128])
    Erow = din("Erow", [2, T])
    Cband_in = din("CbandC", [128, 896])
    xn_spill = nc.dram_tensor("xn_spill", [128, NS * T], BF16, kind="Internal").ap()
    sums_spill = nc.dram_tensor("sums_spill", [16, T], F32, kind="Internal").ap()
    bc_spill = nc.dram_tensor("bc_spill", [16, 2 * T], BF16, kind="Internal").ap()
    bc_rep = nc.dram_tensor("bc_rep", [128, 16 * 2 * T], BF16, kind="Internal").ap()

    partial = dout("partial", [T, D])
    xn1f = dout("xn1f", [128, NS * T], BF16)
    R_out = dout("R_out", [T, E1])
    p_out = dout("p_out", [T, E1])
    dbg = {}

    def dbg_out(name, shape, dt=F32):
        dbg[name] = dout("dbg_" + name, shape, dt)
        return dbg[name]

    sb = lambda n, s, d: stk.enter_context(nc.sbuf_tensor(n, s, d))
    T1 = sb("T1", [128, 2 * NS * T], BF16)
    T1f = T1.ap().bitcast(F32)
    T2 = sb("T2", [128, NT * H * 65], BF16)
    T2f = T2.ap().bitcast(F32)
    T3 = sb("T3", [128, NS * T], BF16)
    T5 = sb("T5", [128, 4096], BF16)
    TQ = sb("TQ", [128, 4096], BF16)
    TK = sb("TK", [128, 4096], BF16)
    T7 = sb("T7", [128, 1024], BF16)
    T8 = sb("T8", [128, 896], F32)
    T9 = sb("T9", [128, 2048], F32)
    T10 = sb("T10", [128, 4096], BF16)
    T11 = sb("T11", [128, 2048], F32)
    T12 = sb("T12", [128, 2048], F32)
    T12b = T12.ap().bitcast(BF16)
    identb = sb("identb", [128, 128], BF16)
    identf = sb("identf", [128, 128], F32)
    CC = sb("CCt", [128, 16], F32)

    PA = stk.enter_context(nc.psum_tensor("PA", [128, 2048], F32))
    PB = stk.enter_context(nc.psum_tensor("PB", [128, 1024], F32))
    PC = stk.enter_context(nc.psum_tensor("PC", [128, 512], F32))
    PD = stk.enter_context(nc.psum_tensor("PD", [128, 512], F32))
    PCb = PC.ap().bitcast(BF16)
    PAb = PA.ap().bitcast(BF16)
    def pa_bank_bf(c):
        return PAb[:, (c % 4) * 1024:(c % 4) * 1024 + 128]

    ci = p.dma("sync", identf.ap(), ident_in)
    CZ = dict(neg8=0, eps=1, one=2, negone=3, isq2=4, zero=5)
    for k, v in [("neg8", -8.0), ("eps", 1e-5), ("one", 1.0), ("negone", -1.0),
                 ("isq2", 1.0 / math.sqrt(2.0)), ("zero", 0.0)]:
        MSET(CC.ap()[:, CZ[k]:CZ[k] + 1], v)

    def cc(name, n=128, base=0):
        return CC.ap()[base:base + n, CZ[name]:CZ[name] + 1]

    CB_done = p.dma("sync", T8.ap(), Cband_in)
    idb = CP(identb.ap(), identf.ap(), wait=[("dq_sync", ci)])

    if upto == "c0":
        return _finishA(nc, p, stk, dbg)
    xn_stage = TQ.ap()[:, 0:2048]
    for j in range(NT):
        slot = j % 2
        st = T9.ap()[:, slot * 1024:(slot + 1) * 1024]
        cx = p.dma("sync", st, x0[j * 128:(j + 1) * 128, :],
                   wait=p.need(("t9", slot)))
        wx = [("dq_sync", cx)]
        sc = T11.ap()[:, slot * 8: slot * 8 + 8]
        s1, s2, mean, var, rstd, nb, sd = (sc[:, i:i + 1] for i in range(7))
        junk = T12.ap()[:, slot * 1024:(slot + 1) * 1024]
        a1 = ACTF(junk, st, AF.Square, accum=s2, wait=wx + p.need(("lnsc", slot)))
        RED(s1, st, ALU.add, wait=wx)
        TS(mean, s1, 1.0 / D)
        TS(var, s2, 1.0 / D, wait=[("act", a1)])
        m2 = sc[:, 7:8]
        TT(m2, mean, mean, ALU.mult)
        TT(var, var, m2, ALU.subtract)
        a2 = ACTF(sd, var, AF.Sqrt, bias=cc("eps"), wait=[("dve", p.now("dve"))])
        RCP(rstd, sd, wait=[("act", a2)])
        dnb = STT(nb, mean, -1.0, rstd, ALU.mult, ALU.mult)
        if upto == "a1a":
            return _finishA(nc, p, stk, dbg)
        xst = xn_stage[:, slot * 1024:(slot + 1) * 1024]
        d1 = TS(xst, st, rstd, nb, ALU.mult, ALU.add, wait=p.need(("xst", slot)))
        p.used(("t9", slot), "dve", d1)
        p.used(("lnsc", slot), "dve", d1)
        if upto == "a1b":
            return _finishA(nc, p, stk, dbg)
        for c in range(NS):
            pt = pa_bank_bf(c)
            tr = TRN(pt, xst[:, c * 128:(c + 1) * 128], identb.ap(),
                     wait=[("dve", d1), ("dve", idb)] + p.need(("pc", c % 4)))
            ev = CP(T3.ap()[:, c * T + j * 128: c * T + (j + 1) * 128], pt,
                    wait=[("pe", tr)])
            p.used(("pc", c % 4), "dve", ev)
            if upto == "a1c":
                return _finishA(nc, p, stk, dbg)
        p.used(("xst", slot), "pe", p.now("pe"))
    ln_done = p.now("dve")
    if upto == "a1":
        return _finishA(nc, p, stk, dbg)
    sp_done = p.dma("gpsimd", xn_spill, T3.ap(), wait=[("dve", ln_done)])
    if "xn_fm" in debug:
        p.dma("gpsimd", dbg_out("xn_fm", [128, NS * T], BF16), T3.ap(),
              wait=[("dve", ln_done)])
    if upto == "a2":
        return _finishA(nc, p, stk, dbg)

    QB, KB = 0, NS * T

    def w_guard(key, val):
        p.used(key, "pe", val)

    def proj_into(Wsrc, dest):
        for m in range(NS):
            slot = m % 2
            st = T9.ap()[:, slot * 1024:(slot + 1) * 1024]
            wbf = T10.ap()[:, slot * 1024:(slot + 1) * 1024]
            cd = p.dma("sync", st.rearrange("p (c n) -> p c n", n=128),
                       Wsrc[:, m * 128:(m + 1) * 128].rearrange("(c p) n -> p c n", p=128),
                       wait=p.need(("t9", slot)))
            cast = CP(wbf, st, wait=[("dq_sync", cd)] + p.need(("t10", slot)))
            p.used(("t9", slot), "dve", cast)
            last = None
            for jc in range(NJ):
                ps = PA.ap()[:, jc * 512:(jc + 1) * 512]
                w0 = [("dve", cast), ("dve", ln_done)] + p.need(("pa", jc))
                for kc in range(NS):
                    last = MM(ps, wbf[:, kc * 128:(kc + 1) * 128],
                              T3.ap()[:, kc * T + jc * 512: kc * T + (jc + 1) * 512],
                              start=(kc == 0), stop=(kc == NS - 1),
                              wait=(w0 if kc == 0 else ()))
                ev = CP(dest(m, jc), ps, wait=[("pe", last)])
                p.used(("pa", jc), "dve", ev)
            w_guard(("t10", slot), p.now("pe"))

    proj_into(Wq, lambda m, jc: T1.ap()[:, QB + m * T + jc * 512: QB + m * T + (jc + 1) * 512])
    KV = T2.ap()[:, 0:NS * T]
    proj_into(Wk, lambda m, jc: KV[:, m * T + jc * 512: m * T + (jc + 1) * 512])
    kv_done = p.now("dve")

    wdkv_bf = T10.ap()[:, 2048:2048 + 8 * DL]
    st = T9.ap()[:, 0:1024]
    cd = p.dma("sync", st.rearrange("p (c n) -> p c n", n=DL),
               Wdkv.rearrange("(c p) n -> p c n", p=128)[:, 0:4, :],
               wait=p.need(("t9", 0)))
    c1 = CP(wdkv_bf[:, 0:4 * DL], st, wait=[("dq_sync", cd)])
    st2 = T9.ap()[:, 1024:2048]
    cd2 = p.dma("sync", st2.rearrange("p (c n) -> p c n", n=DL),
                Wdkv.rearrange("(c p) n -> p c n", p=128)[:, 4:8, :],
                wait=p.need(("t9", 1)))
    c2 = CP(wdkv_bf[:, 4 * DL:8 * DL], st2, wait=[("dq_sync", cd2)])
    p.used(("t9", 0), "dve", c1)
    p.used(("t9", 1), "dve", c2)
    wd_done = p.now("dve")

    def latent_pass(first):
        for ls in range(2):
            for jc in range(NJ):
                ps = PB.ap()[:, (jc % 2) * 512:((jc % 2) + 1) * 512]
                w0 = [("dve", wd_done), ("dve", p.now("dve"))] + p.need(("pb", jc % 2))
                for kc in range(NS):
                    last = MM(ps, wdkv_bf[:, kc * DL + ls * 128: kc * DL + (ls + 1) * 128],
                              KV[:, kc * T + jc * 512: kc * T + (jc + 1) * 512],
                              start=(kc == 0), stop=(kc == NS - 1),
                              wait=(w0 if kc == 0 else ()))
                sl = T5.ap()[:, ls * T + jc * 512: ls * T + (jc + 1) * 512]
                if first:
                    ev = CP(sl, ps, wait=[("pe", last)])
                else:
                    ev = TT(sl, sl, ps, ALU.add, wait=[("pe", last)])
                p.used(("pb", jc % 2), "dve", ev)

    latent_pass(True)
    lat_k = p.now("dve")
    proj_into(Wv, lambda m, jc: KV[:, m * T + jc * 512: m * T + (jc + 1) * 512])
    latent_pass(False)
    lat_done = p.now("dve")

    wup_bf = T10.ap()[:, 0:2048]
    for kc in range(2):
        slot = kc % 2
        st = T9.ap()[:, slot * 1024:(slot + 1) * 1024]
        cd = p.dma("sync", st, Wupk[kc * 128:(kc + 1) * 128, :], wait=p.need(("t9", slot)))
        cw = CP(wup_bf[:, kc * 1024:(kc + 1) * 1024], st,
                wait=[("dq_sync", cd)] + p.need(("t10", slot)))
        p.used(("t9", slot), "dve", cw)
    wupk_done = p.now("dve")
    for m in range(NS):
        for jc in range(NJ):
            ps = PA.ap()[:, jc * 512:(jc + 1) * 512]
            w0 = [("dve", wupk_done), ("dve", lat_done)] + p.need(("pa", jc))
            for kc in range(2):
                last = MM(ps, wup_bf[:, kc * 1024 + m * 128: kc * 1024 + (m + 1) * 128],
                          T5.ap()[:, kc * T + jc * 512: kc * T + (jc + 1) * 512],
                          start=(kc == 0), stop=(kc == 1), wait=(w0 if kc == 0 else ()))
            ev = CP(T1.ap()[:, KB + m * T + jc * 512: KB + m * T + (jc + 1) * 512], ps,
                    wait=[("pe", last)])
            p.used(("pa", jc), "dve", ev)
    krec_done = p.now("dve")

    wupv_bf = T10.ap()[:, 2048 + 8 * DL: 2048 + 8 * DL + 0]
    wupv_bf = T10.ap()[:, 0:2048]
    vh = T2.ap().rearrange("p (i h c) -> p i h c", i=NT, h=H)
    ones_done = MSET(vh[:, :, :, 64:65], 1.0, wait=[("dve", lat_done),
                                                    ("pe", p.now("pe"))])
    for kc in range(2):
        slot = kc % 2
        st = T9.ap()[:, slot * 1024:(slot + 1) * 1024]
        cd = p.dma("sync", st, Wupv[kc * 128:(kc + 1) * 128, :], wait=p.need(("t9", slot)))
        cw = CP(wupv_bf[:, kc * 1024:(kc + 1) * 1024], st,
                wait=[("dq_sync", cd), ("pe", p.now("pe"))])
        p.used(("t9", slot), "dve", cw)
    wupv_done = p.now("dve")
    for i in range(NT):
        ps = PB.ap()
        w0 = [("dve", wupv_done), ("dve", ones_done)] + p.need(("pb", 0)) + p.need(("pb", 1))
        for kc in range(2):
            for n in range(2):
                last = MM(ps[:, n * 512:(n + 1) * 512],
                          T5.ap()[:, kc * T + i * 128: kc * T + (i + 1) * 128],
                          wupv_bf[:, kc * 1024 + n * 512: kc * 1024 + (n + 1) * 512],
                          start=(kc == 0), stop=(kc == 1),
                          wait=(w0 if (kc == 0 and n == 0) else ()))
        ev = CP(vh[:, i, :, 0:64], ps.rearrange("p (h c) -> p h c", h=H),
                wait=[("pe", last)])
        p.used(("pb", 0), "dve", ev)
        p.used(("pb", 1), "dve", ev)
    vhat_done = p.now("dve")

    e1 = p.dma("sync", T9.ap()[:, 0:T], Erow[0:1, :].broadcast_to([128, T]),
               wait=[("dve", vhat_done), ("dve", krec_done)] +
               p.need(("t9", 0)) + p.need(("t9", 1)))
    ce1 = CP(T5.ap()[:, 0:T], T9.ap()[:, 0:T], wait=[("dq_sync", e1)])
    e2 = p.dma("sync", T9.ap()[:, 0:T], Erow[1:2, :].broadcast_to([128, T]),
               wait=[("dve", ce1)])
    ce2 = CP(T5.ap()[:, T:2 * T], T9.ap()[:, 0:T], wait=[("dq_sync", e2)])
    p.used(("t9", 0), "dve", ce2)
    p.used(("t9", 1), "dve", ce2)
    erep_done = p.now("dve")

    if "qkrec" in debug:
        p.dma("gpsimd", dbg_out("q_fm", [128, NS * T], BF16), T1.ap()[:, QB:QB + NS * T],
              wait=[("dve", krec_done)])
        p.dma("gpsimd", dbg_out("krec_fm", [128, NS * T], BF16),
              T1.ap()[:, KB:KB + NS * T], wait=[("dve", krec_done)])
        p.dma("gpsimd", dbg_out("vhat", [128, NT * H * 65], BF16),
              T2.ap()[:, 0:NT * H * 65], wait=[("dve", vhat_done)])
        p.dma("gpsimd", dbg_out("latent", [128, 2 * T], BF16), T5.ap()[:, 0:2 * T],
              wait=[("dve", lat_done)])
    if upto == "a3":
        return _finishA(nc, p, stk, dbg)

    p.barrier()
    EP = lambda rb: T5.ap()[rb:rb + 64, 0:T]
    EM = lambda rb: T5.ap()[rb:rb + 64, T:2 * T]
    for h in range(H):
        sl, rb = h // 2, (h % 2) * 64
        q_src = T1.ap()[rb:rb + 64, QB + sl * T: QB + (sl + 1) * T]
        k_src = T1.ap()[rb:rb + 64, KB + sl * T: KB + (sl + 1) * T]
        qA = TQ.ap()[rb:rb + 64, 0:T]
        qB = TQ.ap()[rb:rb + 64, T:2 * T]
        kA = TK.ap()[rb:rb + 64, 0:T]
        kB = TK.ap()[rb:rb + 64, T:2 * T]
        wq = p.need(("qcopy", h % 2)) + [("dve", erep_done)]
        TT(qA, q_src, EP(rb), ALU.mult, wait=wq)
        TT(qB, q_src, EM(rb), ALU.mult)
        TT(kA, k_src, EM(rb), ALU.mult)
        cpq = TT(kB, k_src, EP(rb), ALU.mult)
        T9b16 = T9.ap().bitcast(BF16)
        exslot = [T7.ap()[:, 0:512], T7.ap()[:, 512:1024],
                  T9b16[:, 2048:2560], T9b16[:, 2560:3072]]
        sc_ps = lambda j: PA.ap()[:, j * 512:(j + 1) * 512]
        pvs = [PB.ap()[0:65, 0:512], PB.ap()[0:65, 512:1024],
               PC.ap()[0:65, :], PD.ap()[0:65, :]]
        for i in range(NT):
            jstr = i // 4
            for j in range(NJ):
                r = i - 4 * j
                kx, qx = (kA, qA) if r >= 0 else (kB, qB)
                mmw = [("dve", cpq)] + p.need(("ps", j))
                mq = MM(sc_ps(j), kx[:, i * 128:(i + 1) * 128],
                        qx[:, j * 512:(j + 1) * 512],
                        start=True, stop=True, wait=mmw)
                p.used(("mq", j), "pe", mq)
            mqlast = p.guard[("mq", 3)][1]
            sc = T9.ap()[:, 0:512]
            o_r = 384 - 128 * (i - 4 * jstr)
            dv = TT(sc, sc_ps(jstr), T8.ap()[:, o_r:o_r + 512], ALU.mult,
                    wait=[("pe", mqlast), ("dq_sync", CB_done)] + p.need(("t9s",)))
            aes = {}
            opair = (jstr // 2) ^ 1
            j0, j1 = 2 * opair, 2 * opair + 1
            exp_pair = (T7.ap()[:, 0:1024] if opair == 0 else T9b16[:, 2048:3072])
            wexp = p.need(("ex", j0)) + p.need(("ex", j1))
            aep = ACTF(exp_pair, PA.ap()[:, j0 * 512:(j1 + 1) * 512], AF.Exp,
                       bias=cc("neg8"), wait=[("pe", mqlast)] + wexp)
            p.used(("ps", j0), "act", aep)
            p.used(("ps", j1), "act", aep)
            aes[j0] = aes[j1] = aep
            for j in (2 * (jstr // 2), 2 * (jstr // 2) + 1):
                wex = p.need(("ex", j))
                if j == jstr:
                    ae = ACTF(exslot[j], sc, AF.Exp, bias=cc("neg8"),
                              wait=[("dve", dv)] + wex)
                    p.used(("t9s",), "act", ae)
                    p.used(("ps", j), "dve", dv)
                else:
                    ae = ACTF(exslot[j], sc_ps(j), AF.Exp, bias=cc("neg8"),
                              wait=[("pe", mqlast)] + wex)
                    p.used(("ps", j), "act", ae)
                aes[j] = ae
            for j in range(NJ):
                mpv = MM(pvs[j], T2.ap()[:, i * H * 65 + h * 65: i * H * 65 + (h + 1) * 65],
                         exslot[j], start=(i == 0), stop=(i == NT - 1),
                         wait=[("act", aes[j])] + (p.need(("pbv", j)) if i == 0 else []))
                p.used(("ex", j), "pe", mpv)
        for j in range(NJ):
            ev1 = CP(T3.ap()[rb:rb + 64, sl * T + j * 512: sl * T + (j + 1) * 512],
                     pvs[j][0:64, :],
                     wait=[("pe", p.now("pe")), ("dq_gpsimd", sp_done)])
            stg = j % 2
            ev2 = CP(T12.ap()[64:65, 1024 + stg * 512: 1024 + (stg + 1) * 512],
                     pvs[j][64:65, :],
                     wait=p.need(("sstage", stg)))
            p.used(("pbv", j), "dve", ev2)
            cdd = p.dma("gpsimd", T12.ap()[h:h + 1, j * 512:(j + 1) * 512],
                        T12.ap()[64:65, 1024 + stg * 512: 1024 + (stg + 1) * 512],
                        wait=[("dve", ev2)])
            p.used(("sstage", stg), "dq_gpsimd", cdd)
        p.used(("qcopy", h % 2), "pe", p.now("pe"))
    sums_done = p.now("dq_gpsimd")
    p.used(("t9w", 0), "act", p.now("act"))
    p.used(("t9w", 1), "act", p.now("act"))
    rc = RCP(T11.ap()[0:16, 0:T], T12.ap()[0:16, 0:T],
             wait=[("dq_gpsimd", sums_done), ("dve", p.now("dve"))])
    rsp = p.dma("gpsimd", sums_spill, T11.ap()[0:16, 0:T], wait=[("dve", rc)])
    for s in range(NS):
        rep = T9.ap()
        cdr = p.dma("gpsimd", rep[0:64, :], sums_spill[2 * s:2 * s + 1, :].broadcast_to([64, T]),
                    wait=p.need(("t9rep",)))
        cdr2 = p.dma("gpsimd", rep[64:128, :],
                     sums_spill[2 * s + 1:2 * s + 2, :].broadcast_to([64, T]))
        nm = TT(T3.ap()[:, s * T:(s + 1) * T], T3.ap()[:, s * T:(s + 1) * T], rep,
                ALU.mult, wait=[("dq_gpsimd", cdr2)])
        p.used(("t9rep",), "dve", nm)
    attn_done = p.now("dve")

    if "attn" in debug:
        p.dma("gpsimd", dbg_out("attn_fm", [128, NS * T], BF16), T3.ap(),
              wait=[("dve", attn_done)])
        p.dma("gpsimd", dbg_out("sums", [128, T]), T12.ap()[:, 0:T],
              wait=[("dve", attn_done)])
    if upto == "a4":
        return _finishA(nc, p, stk, dbg)

    p.barrier()
    for j in range(NT):
        p.dma("sync", T1f[:, j * 1024:(j + 1) * 1024], x0[j * 128:(j + 1) * 128, :])
    x0_done = p.now("dq_sync")
    for half in range(2):
        wo_bf = T10.ap()[:, 0:4096]
        cast_done = []
        for kc in range(NS):
            slot = kc % 2
            st = T9.ap()[:, slot * 1024: slot * 1024 + 512]
            cd = p.dma("sync", st, Wo[kc * 128:(kc + 1) * 128, half * 512:(half + 1) * 512],
                       wait=p.need(("t9w", slot)))
            cw = CP(wo_bf[:, kc * 512:(kc + 1) * 512], st,
                    wait=[("dq_sync", cd)] + ([("pe", p.now("pe"))] if half else []))
            p.used(("t9w", slot), "dve", cw)
        wo_done = p.now("dve")
        for j in range(NT):
            ps = PC.ap() if j % 2 == 0 else PD.ap()
            w0 = [("dve", wo_done), ("dve", attn_done)] + p.need(("psw", j % 2))
            for kc in range(NS):
                last = MM(ps, T3.ap()[:, kc * T + j * 128: kc * T + (j + 1) * 128],
                          wo_bf[:, kc * 512:(kc + 1) * 512],
                          start=(kc == 0), stop=(kc == NS - 1),
                          wait=(w0 if kc == 0 else ()))
            dst = T1f[:, j * 1024 + half * 512: j * 1024 + (half + 1) * 512]
            ad = TT(dst, dst, ps, ALU.add, wait=[("pe", last), ("dq_sync", x0_done)])
            p.used(("psw", j % 2), "dve", ad)
    x1_attn = p.now("dve")
    if "x1a" in debug:
        p.dma("gpsimd", dbg_out("x1a", [128, NT * 1024]), T1f[:, 0:NT * 1024],
              wait=[("dve", x1_attn)])
    if upto == "a4b":
        return _finishA(nc, p, stk, dbg)

    p.barrier()
    xn_rl = p.dma("sync", T3.ap(), xn_spill)
    wg_bf = T10.ap()[:, 0:NS * E0]
    st = T9.ap()[:, 0:NS * E0]
    cd = p.dma("sync", st.rearrange("p (c n) -> p c n", n=E0),
               WgT.rearrange("(c p) n -> p c n", p=128))
    cwg = CP(wg_bf, st, wait=[("dq_sync", cd)])
    p.used(("mt9a",), "dve", cwg)
    we_tm = T11.ap()[:, 0:NT * E0]
    gsc = T11.ap()[:, 300:396]
    for j in range(NT):
        ps = PD.ap()[:, 0:E0]
        w0 = [("dve", cwg), ("dq_sync", xn_rl)] + p.need(("pdg",))
        for kc in range(NS):
            last = MM(ps, T3.ap()[:, kc * T + j * 128: kc * T + (j + 1) * 128],
                      wg_bf[:, kc * E0:(kc + 1) * E0],
                      start=(kc == 0), stop=(kc == NS - 1), wait=(w0 if kc == 0 else ()))
        sg = T12.ap()[:, 1024:1024 + 16]
        sgw = p.need(("gsc",))
        a1 = ACTF(sg, ps, AF.Sigmoid, wait=[("pe", last)] + sgw)
        p.used(("pdg",), "act", a1)
        gmax = gsc[:, 0:4]; m1 = gsc[:, 4:5]; gex = gsc[:, 5:9]; m2 = gsc[:, 9:10]
        gsel = gsc[:, 10:14]; notg = gsc[:, 14:18]; sme = gsc[:, 18:34]
        sv = gsc[:, 34:50]; m4 = gsc[:, 50:51]; wr = gsc[:, 51:52]
        wi = gsc[:, 52:53]; ws = gsc[:, 53:54]
        keep = gsc[:, 54:70]; msk = gsc[:, 70:86]; mi = gsc[:, 86:87]
        RED(gmax, sg.rearrange("p (g e) -> p g e", g=4), ALU.max, wait=[("act", a1)])
        RED(m1, gmax, ALU.max)
        TS(gex, gmax, m1, None, ALU.is_ge)
        STT(gex, gex, -1e9, gmax, ALU.mult, ALU.add)
        RED(m2, gex, ALU.max)
        TS(gsel, gmax, m2, None, ALU.is_ge)
        TS(notg, gsel, -1.0, 1.0, ALU.mult, ALU.add)
        ng_b = notg.unsqueeze(2).broadcast_to([128, 4, 4])
        STT(sme.rearrange("p (a b) -> p a b", b=4), ng_b, -1e9,
            sg.rearrange("p (a b) -> p a b", b=4), ALU.mult, ALU.add)
        CP(sv, sme)
        for it in range(3):
            RED(mi, sv, ALU.max)
            TS(msk, sv, mi, None, ALU.is_ge)
            STT(sv, msk, -1e9, sv, ALU.mult, ALU.add)
        RED(m4, sv, ALU.max)
        TS(keep, sme, m4, None, ALU.is_ge)
        wvals = T12.ap()[:, 1040:1056]
        TT(wvals, sg, keep, ALU.mult)
        RED(wr, wvals, ALU.add)
        RCP(wi, wr)
        TS(ws, wi, 2.5)
        gd = TS(we_tm[:, j * E0:(j + 1) * E0], wvals, ws, None, ALU.mult)
        p.used(("gsc",), "dve", gd)
        pt = PC.ap()[0:16, 128:256]
        tr = TRN(pt, we_tm[:, j * E0:(j + 1) * E0], identf.ap(),
                 wait=[("dve", gd)] + p.need(("pdt",)))
        evt = CP(TK.ap()[0:16, j * 128:(j + 1) * 128], pt, wait=[("pe", tr)])
        p.used(("pdt",), "dve", evt)
    gate_done = p.now("dve")
    if "gate" in debug:
        p.dma("gpsimd", dbg_out("we_tm", [128, NT * E0]), we_tm, wait=[("dve", gate_done)])
    if upto == "a5g":
        return _finishA(nc, p, stk, dbg)

    b2_bf = TK.ap()[0:16, 2048:2048 + D]
    st = T9.ap()[0:16, 0:D]
    cdb2 = p.dma("sync", st, b2, wait=p.need(("mt9a",)))
    cb2 = CP(b2_bf, st, wait=[("dq_sync", cdb2)])
    p.used(("mt9a",), "dve", cb2)
    for j in range(NT):
        ps = PB.ap()
        w0 = [("dve", cb2), ("dve", gate_done)] + p.need(("pbb",))
        m1 = MM(ps[:, 0:512], TK.ap()[0:16, j * 128:(j + 1) * 128],
                b2_bf[0:16, 0:512], start=True, stop=True, wait=w0)
        m2 = MM(ps[:, 512:1024], TK.ap()[0:16, j * 128:(j + 1) * 128],
                b2_bf[0:16, 512:1024], start=True, stop=True)
        dst = T1f[:, j * 1024:(j + 1) * 1024]
        ad = TT(dst, dst, ps, ALU.add, wait=[("pe", m2)])
        p.used(("pbb",), "dve", ad)
    b2fold_pe = p.now("pe")
    p.used(("mw2", 4), "pe", b2fold_pe)
    p.used(("mw2", 5), "pe", b2fold_pe)
    p.used(("mw2", 6), "pe", b2fold_pe)
    p.used(("mw2", 7), "pe", b2fold_pe)
    for e in range(E0):
        bslot = e % 2
        bc = T11.ap()[:, 400 + bslot * 16: 400 + bslot * 16 + 16]
        cdb3 = p.dma("sync", bc, b13c[e], wait=p.need(("bc", bslot)))
        for isl in range(NS):
            wslot = isl % 2
            s1sl = T9.ap()[:, 0:1024]
            s3sl = T9.ap()[:, 1024:2048]
            w1c = T10.ap()[:, wslot * 1024:(wslot + 1) * 1024]
            w3c = T10.ap()[:, 2048 + wslot * 1024: 2048 + (wslot + 1) * 1024]
            cd1 = p.dma("sync", s1sl.rearrange("p (c n) -> p c n", n=128),
                        W1T[e][:, isl * 128:(isl + 1) * 128].rearrange("(c p) n -> p c n", p=128),
                        wait=p.need(("mt9a",)))
            cw1 = CP(w1c, s1sl, wait=[("dq_sync", cd1)] + p.need(("mw1", wslot)))
            p.used(("mt9a",), "dve", cw1)
            cd3 = p.dma("sync", s3sl.rearrange("p (c n) -> p c n", n=128),
                        W3T[e][:, isl * 128:(isl + 1) * 128].rearrange("(c p) n -> p c n", p=128),
                        wait=p.need(("mt9b",)))
            cw3 = CP(w3c, s3sl, wait=[("dq_sync", cd3)] + p.need(("mw3", wslot)))
            p.used(("mt9b",), "dve", cw3)
            for jc in range(NJ):
                ph1 = PC.ap()
                ph3 = PD.ap()
                w0 = [("dve", cw1), ("dq_sync", cdb3), ("dq_sync", xn_rl)] + \
                    p.need(("ph1",))
                w3w = [("dve", cw3)] + p.need(("ph3",))
                for kc in range(NS):
                    l1 = MM(ph1, w1c[:, kc * 128:(kc + 1) * 128],
                            T3.ap()[:, kc * T + jc * 512: kc * T + (jc + 1) * 512],
                            start=(kc == 0), stop=(kc == NS - 1),
                            wait=(w0 if kc == 0 else ()))
                for kc in range(NS):
                    l3 = MM(ph3, w3c[:, kc * 128:(kc + 1) * 128],
                            T3.ap()[:, kc * T + jc * 512: kc * T + (jc + 1) * 512],
                            start=(kc == 0), stop=(kc == NS - 1),
                            wait=(w3w if kc == 0 else ()))
                s1 = T12b[:, 2048 + (jc % 2) * 512: 2048 + (jc % 2 + 1) * 512]
                aw = p.need(("silu", jc % 2)) + [("dve", gate_done)]
                a1 = ACTF(s1, ph1, AF.Silu, bias=bc[:, isl:isl + 1], wait=[("pe", l1)] + aw)
                p.used(("ph1",), "act", a1)
                hdst = T2.ap()[:, isl * T + jc * 512:isl * T + (jc + 1) * 512]
                hd = STT(hdst, ph3, bc[:, 8 + isl:9 + isl], s1, ALU.add, ALU.mult,
                         wait=[("pe", l3), ("act", a1), ("dve", cb2)] + p.need(("hbuf", isl, jc)))
                p.used(("ph3",), "dve", hd)
                p.used(("silu", jc % 2), "dve", hd)
            w_guard(("mw1", wslot), p.now("pe"))
            w_guard(("mw3", wslot), p.now("pe"))
        h_done = p.now("dve")
        h_pe = p.now("pe")
        w2loc = []
        for isl in range(NS):
            wslot = isl % 2
            st1 = T9.ap()[:, wslot * 1024:(wslot + 1) * 1024]
            dstw = (T10.ap() if isl < 4 else TK.ap())[:, (isl % 4) * 1024:
                                                       ((isl % 4) + 1) * 1024]
            w2loc.append(dstw)
            key9 = ("mt9a",) if wslot == 0 else ("mt9b",)
            cd2 = p.dma("sync", st1, W2T[e][isl * 128:(isl + 1) * 128, :],
                        wait=p.need(key9))
            cw2 = CP(dstw, st1,
                     wait=[("dq_sync", cd2), ("pe", h_pe)] + p.need(("mw2", isl)) +
                     ([("dve", cb2)] if (e == 0 and isl < 2) else []))
            p.used(key9, "dve", cw2)
        w2_done = p.now("dve")
        for j in range(NT):
            ps = PA.ap()[:, (j % 2) * 1024:(j % 2 + 1) * 1024]
            w0 = [("dve", w2_done), ("dve", h_done)] + p.need(("pay", j % 2))
            for isl in range(NS):
                for n in range(2):
                    last = MM(ps[:, n * 512:(n + 1) * 512],
                              T2.ap()[:, isl * T + j * 128: isl * T + (j + 1) * 128],
                              w2loc[isl][:, n * 512:(n + 1) * 512],
                              start=(isl == 0), stop=(isl == NS - 1),
                              wait=(w0 if (isl == 0 and n == 0) else ()))
            dst = T1f[:, j * 1024:(j + 1) * 1024]
            ac = STT(dst, ps, we_tm[:, j * E0 + e: j * E0 + e + 1], dst,
                     ALU.mult, ALU.add, wait=[("pe", last)])
            p.used(("pay", j % 2), "dve", ac)
        for isl in range(NS):
            p.used(("mw2", isl), "pe", p.now("pe"))
        p.used(("bc", bslot), "dve", p.now("dve"))
        p.used(("mw1", 0), "pe", p.now("pe"))
        p.used(("mw1", 1), "pe", p.now("pe"))
        p.used(("mw3", 0), "pe", p.now("pe"))
        p.used(("mw3", 1), "pe", p.now("pe"))
        for isl in range(NS):
            for jc in range(NJ):
                p.used(("hbuf", isl, jc), "pe", p.now("pe"))
    x1_done = p.now("dve")
    if "x1" in debug:
        p.dma("gpsimd", dbg_out("x1", [128, NT * 1024]), T1f[:, 0:NT * 1024],
              wait=[("dve", x1_done)])
    if upto == "a5":
        return _finishA(nc, p, stk, dbg)

    p.barrier()
    xn_stage2 = TQ.ap()[:, 0:2048]
    for j in range(NT):
        slot = j % 2
        src = T1f[:, j * 1024:(j + 1) * 1024]
        sc = T11.ap()[:, 600 + slot * 8: 600 + slot * 8 + 8]
        s1, s2, mean, var, rstd, nb, sd = (sc[:, i:i + 1] for i in range(7))
        junk = T12.ap()[:, slot * 1024:(slot + 1) * 1024]
        a1 = ACTF(junk, src, AF.Square, accum=s2, wait=p.need(("lnsc2", slot)))
        RED(s1, src, ALU.add)
        TS(mean, s1, 1.0 / D)
        TS(var, s2, 1.0 / D, wait=[("act", a1)])
        m2 = sc[:, 7:8]
        TT(m2, mean, mean, ALU.mult)
        TT(var, var, m2, ALU.subtract)
        a2 = ACTF(sd, var, AF.Sqrt, bias=cc("eps"), wait=[("dve", p.now("dve"))])
        RCP(rstd, sd, wait=[("act", a2)])
        STT(nb, mean, -1.0, rstd, ALU.mult, ALU.mult)
        xst = xn_stage2[:, slot * 1024:(slot + 1) * 1024]
        d1 = TS(xst, src, rstd, nb, ALU.mult, ALU.add, wait=p.need(("xst2", slot)))
        p.used(("lnsc2", slot), "dve", d1)
        for c in range(NS):
            pt = pa_bank_bf(c)
            tr = TRN(pt, xst[:, c * 128:(c + 1) * 128], identb.ap(),
                     wait=[("dve", d1)] + p.need(("pc2", c % 4)))
            ev = CP(T3.ap()[:, c * T + j * 128: c * T + (j + 1) * 128], pt,
                    wait=[("pe", tr)])
            p.used(("pc2", c % 4), "dve", ev)
        p.used(("xst2", slot), "pe", p.now("pe"))
    ln1_done = p.now("dve")
    p.dma("gpsimd", xn1f, T3.ap(), wait=[("dve", ln1_done)])

    wg1_bf = T10.ap()[:, 2048:2048 + NS * E1]
    st = T9.ap()[:, 0:NS * E1]
    cdg = p.dma("sync", st.rearrange("p (c n) -> p c n", n=E1),
                gwT.rearrange("(c p) n -> p c n", p=128), wait=p.need(("mt9a",)))
    cwg1 = CP(wg1_bf, st, wait=[("dq_sync", cdg)])
    probs_tm = T11.ap()[:, 0:NT * E1]
    R_tm = T11.ap()[:, 128:128 + NT * E1]
    for j in range(NT):
        ps = PD.ap()[:, 0:E1]
        w0 = [("dve", cwg1), ("dve", ln1_done)] + p.need(("pdg2",))
        for kc in range(NS):
            last = MM(ps, T3.ap()[:, kc * T + j * 128: kc * T + (j + 1) * 128],
                      wg1_bf[:, kc * E1:(kc + 1) * E1],
                      start=(kc == 0), stop=(kc == NS - 1), wait=(w0 if kc == 0 else ()))
        g = T11.ap()[:, 400:472]
        mrow = g[:, 0:1]; nm = g[:, 1:2]; ssum = g[:, 2:3]; sinv = g[:, 3:4]
        ex = T12.ap()[:, 1024:1032]
        gw2 = p.need(("gsc2",))
        mx = RED(mrow, ps, ALU.max, wait=[("pe", last)] + gw2)
        TS(nm, mrow, -1.0)
        ae = ACTF(ex, ps, AF.Exp, bias=nm, wait=[("dve", p.now("dve"))] + p.need(("ex2",)))
        p.used(("pdg2",), "act", ae)
        RED(ssum, ex, ALU.add, wait=[("act", ae)])
        RCP(sinv, ssum)
        pj = probs_tm[:, j * E1:(j + 1) * E1]
        TS(pj, ex, sinv, None, ALU.mult)
        p.used(("ex2",), "dve", p.now("dve"))
        pe_ = pj.unsqueeze(2).broadcast_to([128, E1, E1])
        pep = pj.unsqueeze(1).broadcast_to([128, E1, E1])
        gt = T12.ap()[:, 1032:1096]
        contrib = T12.ap()[:, 1096:1160]
        gt3 = gt.rearrange("p (a b) -> p a b", b=E1)
        c3 = contrib.rearrange("p (a b) -> p a b", b=E1)
        TT(gt3, pep, pe_, ALU.is_gt)
        TT(c3, gt3, pep, ALU.mult)
        G = g[:, 4:12]; rnk = g[:, 12:20]; asn = g[:, 20:28]
        t1 = g[:, 28:36]; t2 = g[:, 36:44]; pr = g[:, 44:52]
        RED(G, contrib.rearrange("p (a b) -> p a b", a=E1), ALU.add)
        RED(rnk, gt.rearrange("p (a b) -> p a b", a=E1), ALU.add)
        TS(rnk, rnk, 1.0, None, ALU.add)
        TS(asn, G, 0.9, None, ALU.is_lt)
        TT(pr, pj, rnk, ALU.subtract)
        TT(t1, pr, asn, ALU.mult)
        TS(t2, asn, 1e9, -1e9, ALU.mult, ALU.add)
        gR = TT(R_tm[:, j * E1:(j + 1) * E1], t1, t2, ALU.add)
        p.used(("gsc2",), "dve", gR)
    gate1_done = p.now("dve")
    p.dma("gpsimd", R_out.rearrange("(j p) e -> p j e", p=128),
          R_tm.rearrange("p (j e) -> p j e", e=E1), wait=[("dve", gate1_done)])
    p.dma("gpsimd", p_out.rearrange("(j p) e -> p j e", p=128),
          probs_tm.rearrange("p (j e) -> p j e", e=E1), wait=[("dve", gate1_done)])
    if upto == "a6":
        return _finishA(nc, p, stk, dbg)

    p.barrier()
    T7f = T7.ap().bitcast(F32)
    Acols = T7f[:, 0:128]
    for s in range(NS):
        p.dma("sync", Acols[:, s * 16:(s + 1) * 16], A_log[s * 128:(s + 1) * 128, :])
    aload = p.now("dq_sync")
    ae = ACTF(Acols, Acols, AF.Exp, wait=[("dq_sync", aload)])
    aneg = TS(Acols, Acols, -1.0, wait=[("act", ae)])
    bdt = T7f[:, 128:136]
    for s in range(NS):
        p.dma("sync", bdt[:, s:s + 1], b_dt[s * 128:(s + 1) * 128].unsqueeze(1))
    dcol = T7f[:, 136:144]
    for s in range(NS):
        p.dma("sync", dcol[:, s:s + 1], Dvec[s * 128:(s + 1) * 128].unsqueeze(1))
    wx_bf = T10.ap()[:, 0:NS * 64]
    st = T9.ap()[:, 0:NS * 64]
    cdx = p.dma("sync", st.rearrange("p (c n) -> p c n", n=64),
                WxdtT.rearrange("(c p) n -> p c n", p=128))
    cwx = CP(wx_bf, st, wait=[("dq_sync", cdx)])
    bxc = T7f[:, 144:145]
    cbx = p.dma("sync", bxc[0:64, :], b_xdt.unsqueeze(1))
    for jc in range(NJ):
        ps = PA.ap()[0:64, jc * 512:(jc + 1) * 512]
        w0 = [("dve", cwx), ("dve", ln1_done)]
        for kc in range(NS):
            last = MM(ps, wx_bf[:, kc * 64:(kc + 1) * 64],
                      T3.ap()[:, kc * T + jc * 512: kc * T + (jc + 1) * 512],
                      start=(kc == 0), stop=(kc == NS - 1), wait=(w0 if kc == 0 else ()))
    xdt_bf = TK.ap()[0:64, 0:T]
    cxb = TS(xdt_bf, PA.ap()[0:64, 0:T], bxc[0:64, :], None, ALU.add,
             wait=[("pe", last), ("dq_sync", cbx)])

    wbc_bf = T10.ap()[:, 1024:1024 + NS * 32]
    st = T9.ap()[:, 1024:1024 + NS * 16]
    cdB = p.dma("sync", st.rearrange("p (c n) -> p c n", n=16),
                WBT.rearrange("(c p) n -> p c n", p=128))
    cwB = CP(wbc_bf[:, 0:NS * 16], st, wait=[("dq_sync", cdB)])
    cdC = p.dma("sync", st.rearrange("p (c n) -> p c n", n=16),
                WCT.rearrange("(c p) n -> p c n", p=128), wait=[("dve", cwB)])
    cwC = CP(wbc_bf[:, NS * 16:NS * 32], st, wait=[("dq_sync", cdC)])
    onesr = T10.ap()[0:1, 1536:1664]
    MSET(onesr, 1.0)
    st_bb = T11.ap()[0:1, 1950:1982]
    cbb = p.dma("sync", st_bb[:, 0:16], b_B.unsqueeze(0))
    cbc = p.dma("sync", st_bb[:, 16:32], b_C.unsqueeze(0))
    cvb = CP(T10.ap()[0:1, 1664:1696], st_bb, wait=[("dq_sync", cbc)])
    bBr = T10.ap()[0:1, 1664:1680]
    bCr = T10.ap()[0:1, 1680:1696]
    B_fm = TK.ap()[0:16, 2048:4096]
    C_fm = T10.ap()[0:16, 2048:4096]
    for j in range(NT):
        for which in range(2):
            ps = PD.ap()[:, 0:16]
            w0 = [("dve", cwC), ("dve", cvb), ("dve", ln1_done)] + p.need(("pdbc",))
            for kc in range(NS):
                MM(ps, T3.ap()[:, kc * T + j * 128: kc * T + (j + 1) * 128],
                   wbc_bf[:, which * NS * 16 + kc * 16:which * NS * 16 + (kc + 1) * 16],
                   start=(kc == 0), stop=False, wait=(w0 if kc == 0 else ()))
            last = MM(ps, onesr, (bBr if which == 0 else bCr)[:, 0:16],
                      start=False, stop=True)
            bcs = T11.ap()[:, 1024 + which * 16: 1040 + which * 16]
            ev = CP(bcs, ps, wait=[("pe", last)] + p.need(("bcs", which)))
            p.used(("pdbc",), "dve", ev)
            pt = PC.ap()[0:16, 128:256]
            tr = TRN(pt, bcs, identf.ap(), wait=[("dve", ev)] + p.need(("pdt2",)))
            dstf = (B_fm if which == 0 else C_fm)[:, j * 128:(j + 1) * 128]
            evt = CP(dstf, pt, wait=[("pe", tr)])
            p.used(("pdt2",), "dve", evt)
            p.used(("bcs", which), "pe", tr)
    bc_done = p.now("dve")

    bsp = p.dma("gpsimd", bc_spill[:, 0:T], B_fm, wait=[("dve", bc_done)])
    csp = p.dma("gpsimd", bc_spill[:, T:2 * T], C_fm, wait=[("dve", bc_done)])
    for n in range(16):
        p.dma("gpsimd", bc_rep[:, n * 2 * T:(n + 1) * 2 * T],
              bc_spill[n:n + 1, :].broadcast_to([128, 2 * T]))
    bcrep_done = p.now("dq_gpsimd")
    wdt_bf = T10.ap()[0:64, 0:1024]
    stw = T9.ap()[0:64, 0:1024]
    cdw = p.dma("sync", stw, WdtT, wait=[("dve", p.now("dve"))])
    cwd = CP(wdt_bf, stw, wait=[("dq_sync", cdw)])

    if "pre_scan" in debug:
        p.dma("gpsimd", dbg_out("xdt", [64, T], BF16), xdt_bf, wait=[("dve", cxb)])
        p.dma("gpsimd", dbg_out("B_fm", [16, T], BF16), B_fm, wait=[("dve", bc_done)])
        p.dma("gpsimd", dbg_out("C_fm", [16, T], BF16), C_fm, wait=[("dve", bc_done)])

    p.barrier()
    dt_bf = lambda sl: T2.ap()[:, sl * T:(sl + 1) * T]
    scr1 = T9.ap()
    scr2 = T12.ap()
    TKf = TK.ap().bitcast(F32)
    for sl in range(NS):
        pss = PA.ap()
        w0 = [("dve", cwd), ("dve", cxb), ("pool", p.now("pool")),
              ("dve", p.now("dve")), ("act", p.now("act"))] + p.need(("pa_scan",))
        for jc in range(NJ):
            last = MM(pss[:, jc * 512:(jc + 1) * 512],
                      wdt_bf[:, sl * 128:(sl + 1) * 128],
                      xdt_bf[:, jc * 512:(jc + 1) * 512],
                      start=True, stop=True, wait=(w0 if jc == 0 else ()))
        bcol = bdt[:, sl:sl + 1]
        a1 = ACTF(scr1, pss, AF.Abs, bias=bcol,
                  wait=[("pe", last)] + p.need(("scr1",)))
        a2 = ACTF(scr2, scr1, AF.Exp, scale=cc("negone"),
                  wait=[("act", a1)] + p.need(("scr2",)))
        a3 = ACTF(scr1, scr2, AF.Ln, bias=cc("one"))
        mx = TS(scr2, pss, bcol, cc("zero"), ALU.add, ALU.max,
                wait=[("act", a3)])
        p.used(("pa_scan",), "dve", mx)
        dtd = TT(dt_bf(sl), scr2, scr1, ALU.add,
                 wait=[("act", a3)] + p.need(("dtb", sl)))
        p.used(("scr1",), "dve", dtd)
        p.used(("scr2",), "dve", dtd)
    dt_done = p.now("dve")

    du = TQ.ap()[:, 0:T]
    T9b16 = T9.ap().bitcast(BF16)
    d0buf = [TK.ap()[:, 0:T], TK.ap()[:, T:2 * T]]
    p.used(("d0b", 0), "dq_gpsimd", csp)
    p.used(("d0b", 1), "dq_gpsimd", csp)
    d1buf = [T12b[:, 0:T], T12b[:, T:2 * T]]
    hbuf = [TQ.ap()[:, T:2 * T], T9b16[:, 0:T]]
    yac = T11.ap()
    for sl in range(NS):
        dud = TT(du, dt_bf(sl), T3.ap()[:, sl * T:(sl + 1) * T], ALU.mult,
                 wait=[("dve", dt_done)] + p.need(("dub",)))
        yid = TS(yac, T3.ap()[:, sl * T:(sl + 1) * T], dcol[:, sl:sl + 1], None,
                 ALU.mult, wait=p.need(("yac",)) + [("pool", p.now("pool"))])
        cBC = [None] * 17
        cBC[0] = p.dma("sync", T5.ap(), bc_rep[:, 0:2 * T],
                       wait=p.need(("bcrep",)) + [("dq_gpsimd", bcrep_done)])
        for n in range(16):
            pn = n % 2
            d0 = ACTF(d0buf[pn], dt_bf(sl), AF.Exp, scale=Acols[:, sl * 16 + n:sl * 16 + n + 1],
                      wait=[("dve", aneg), ("dve", dud)] + p.need(("d0b", pn)))
            d1 = TT(d1buf[pn], du, T5.ap()[:, 0:T], ALU.mult,
                    wait=[("dq_sync", cBC[n])] + p.need(("d1b", pn)))
            hsc = SCAN(hbuf[pn], d0buf[pn], d1buf[pn], 0.0,
                       wait=[("act", d0)] + p.need(("hb", pn)))
            p.used(("d0b", pn), "dve", hsc)
            p.used(("d1b", pn), "dve", hsc)
            hc = TT(d1buf[pn], hbuf[pn], T5.ap()[:, T:2 * T], ALU.mult,
                    wait=[("dve", hsc)])
            p.used(("hb", pn), "dve", hc)
            if n < 15:
                cBC[n + 1] = p.dma("sync", T5.ap(), bc_rep[:, (n + 1) * 2 * T:(n + 2) * 2 * T],
                                   wait=[("dve", d1), ("dve", hc)])
            else:
                p.used(("bcrep",), "dve", hc)
            ya = TT(yac, yac, d1buf[pn], ALU.add, wait=[("dve", hc)] + p.need(("yp",)),
                    eng="gpsimd")
            p.used(("yp",), "pool", ya)
            p.used(("d1b", pn), "pool", ya)
        p.wait("vector", [("pool", p.now("pool"))])
        ybf = T12b[:, 0:T]
        cy = CP(ybf, yac, wait=p.need(("d1b", 0)) + p.need(("ybf",)) +
                [("pool", p.now("pool"))])
        p.used(("yac",), "dve", cy)
        p.used(("dub",), "dve", cy)
        p.used(("dtb", sl), "dve", cy)
        p.used(("d1b", 0), "dve", cy)
        for j in range(NT):
            pt = pa_bank_bf(j)
            tr = TRN(pt, ybf[:, j * 128:(j + 1) * 128], identb.ap(),
                     wait=[("dve", cy)] + p.need(("pc3", j % 4)) +
                     p.need(("pa_scan",)))
            dst = T1f[:, j * 1024 + sl * 128: j * 1024 + (sl + 1) * 128]
            ad = TT(dst, dst, pt, ALU.add, wait=[("pe", tr), ("dve", x1_done)])
            p.used(("pc3", j % 4), "dve", ad)
        p.used(("ybf",), "pe", p.now("pe"))
        p.used(("pa_scan",), "dve", p.now("dve"))
    scan_done = p.now("dve")

    p.dma("gpsimd", partial.rearrange("(j p) d -> p j d", p=128),
          T1f.rearrange("p (j d) -> p j d", d=1024), wait=[("dve", scan_done)])
    if "partial" in debug:
        p.dma("gpsimd", dbg_out("partial2", [128, NT * 1024]), T1f[:, 0:NT * 1024],
              wait=[("dve", scan_done)])
    return _finishA(nc, p, stk, dbg)


def _finishA(nc, p, stk, dbg):
    p.barrier()
    p.build(stk)
    return nc, p, stk, dbg


def _np_t(a):
    return np.ascontiguousarray(np.swapaxes(np.asarray(a), -1, -2))


def prep_constants():
    t = np.arange(T, dtype=np.float64)
    s = 1.0 / math.sqrt(math.sqrt(1024.0))
    s = (1.0 / 32.0) ** 0.5
    Erow = np.stack([np.exp(LAM * t) * s, np.exp(-LAM * t) * s]).astype(np.float32)
    pcol = np.arange(128, dtype=np.float64)[:, None]
    x = np.arange(896, dtype=np.float64)[None, :]
    Cband = np.minimum(1.0, np.exp(-2.0 * LAM * (x - 384.0 - pcol))).astype(np.float32)
    ident = np.eye(128, dtype=np.float32)
    return dict(identc=ident, Erow=Erow, CbandC=Cband)


def prep_A(inputs, core):
    g = lambda n: np.asarray(inputs[n])
    m = dict(
        x0=np.ascontiguousarray(g("L0_attn_x")[core]),
        Wq=g("L0_attn_W_q"), Wk=g("L0_attn_W_k"), Wv=g("L0_attn_W_v"),
        Wo=g("L0_attn_W_o"), Wdkv=g("L0_attn_W_down_kv"),
        Wupk=g("L0_attn_W_up_k"), Wupv=g("L0_attn_W_up_v"),
        WgT=_np_t(g("L0_ffn_Wg")),
        W1T=_np_t(g("L0_ffn_W1")), W3T=_np_t(g("L0_ffn_W3")), W2T=_np_t(g("L0_ffn_W2")),
        b13c=np.concatenate([
            g("L0_ffn_b1").reshape(E0, 8, 128).transpose(0, 2, 1),
            g("L0_ffn_b3").reshape(E0, 8, 128).transpose(0, 2, 1)], axis=2),
        b2=g("L0_ffn_b2"),
        A_log=g("L1_attn_A_log"),
        WxdtT=_np_t(g("L1_attn_W_xdt")), b_xdt=g("L1_attn_b_xdt"),
        WdtT=_np_t(g("L1_attn_W_dt")), b_dt=g("L1_attn_b_dt"),
        WBT=_np_t(g("L1_attn_W_B")), b_B=g("L1_attn_b_B"),
        WCT=_np_t(g("L1_attn_W_C")), b_C=g("L1_attn_b_C"),
        Dvec=g("L1_attn_D"),
        gwT=_np_t(g("L1_ffn_gate_w")),
    )
    m.update(prep_constants())
    return {k: np.ascontiguousarray(np.asarray(v, dtype=np.float32)) for k, v in m.items()}


def build_progB(debug=()):
    nc = bass.Bass("TRN2", target_bir_lowering=False, debug=False)
    stk = ExitStack()
    p = Prog(nc)
    MM, TRN, CP, TT, TS, STT, ACTF, RED, SCAN, SEL, RCP, MSET = _mk_helpers(p)

    din = lambda n, s, d=F32: nc.dram_tensor(n, s, d, kind="ExternalInput").ap()
    dout = lambda n, s, d=F32: nc.dram_tensor(n, s, d, kind="ExternalOutput").ap()

    partial = din("partial", [T, D])
    xn1f = din("xn1f", [128, NS * T], BF16)
    R_all = din("R_all", [8 * T, E1])
    R_my = din("R_my", [T, E1])
    p_my = din("p_my", [T, E1])
    w1T = din("w1T", [E1, D, F1])
    w2T = din("w2T", [E1, F1, D])
    out = dout("out_f", [T, D])
    dbg = {}

    def dbg_out(name, shape, dt=F32):
        dbg[name] = dout("dbg_" + name, shape, dt)
        return dbg[name]

    sb = lambda n, s, d: stk.enter_context(nc.sbuf_tensor(n, s, d))
    X1 = sb("X1", [128, NT * 1024], F32)
    T3 = sb("T3b", [128, NS * T], BF16)
    T2 = sb("T2b", [128, NS * T], BF16)
    T9 = sb("T9b", [128, 2048], F32)
    T10 = sb("T10b", [128, 4096], BF16)
    TK = sb("TKb", [128, 4096], BF16)
    T11 = sb("T11b", [128, 2048], F32)
    T12 = sb("T12b2", [128, 2048], F32)
    CC = sb("CCb", [128, 16], F32)

    PA = stk.enter_context(nc.psum_tensor("PAb", [128, 2048], F32))
    PB = stk.enter_context(nc.psum_tensor("PBb", [128, 1024], F32))
    PC = stk.enter_context(nc.psum_tensor("PCb2", [128, 512], F32))
    PD = stk.enter_context(nc.psum_tensor("PDb", [128, 512], F32))

    MSET(CC.ap()[:, 0:1], 1.0 / math.sqrt(2.0))
    isq2 = CC.ap()[:, 0:1]
    ones_col = CC.ap()[:, 1:2]
    MSET(ones_col, 1.0)
    onesr = CC.ap()[0:1, 2:3]
    cx = p.dma("sync", X1.ap().rearrange("p (j d) -> p j d", d=1024),
               partial.rearrange("(j p) d -> p j d", p=128))
    cxn = p.dma("sync", T3.ap(), xn1f)
    R_re = T11.ap()[:, 0:1024]
    crall = p.dma("sync", R_re.rearrange("p (c e) -> p c e", e=E1),
                  R_all.rearrange("(c p) e -> p c e", p=128))
    sel_tm = T11.ap()[:, 1024:1024 + NT * E1]
    Rm = T12.ap()[:, 0:NT * E1]
    pm = T12.ap()[:, 256:256 + NT * E1]
    crm = p.dma("sync", Rm.rearrange("p (j e) -> p j e", e=E1),
                R_my.rearrange("(j p) e -> p j e", p=128))
    cpm = p.dma("sync", pm.rearrange("p (j e) -> p j e", e=E1),
                p_my.rearrange("(j p) e -> p j e", p=128))

    lo = T12.ap()[0:1, 512:520]
    hi = T12.ap()[0:1, 520:528]
    mid = T12.ap()[0:1, 528:536]
    cnt = T12.ap()[0:1, 536:544]
    cgt = T12.ap()[0:1, 544:552]
    ones_row = T12.ap()[0:1, 552:680]
    MSET(lo, -10.0)
    MSET(hi, 2.0)
    MSET(ones_row, 1.0)
    mrep = T12.ap()[:, 768:776]
    ge = T9.ap()[:, 0:1024]
    cntpc = T12.ap()[:, 784:792]
    for it in range(34):
        TT(mid, lo, hi, ALU.add)
        TS(mid, mid, 0.5)
        mm1 = MM(PD.ap()[:, 0:E1], ones_row, mid, start=True, stop=True,
                 wait=p.need(("pdrep",)) + [("dve", p.now("dve"))])
        evr = CP(mrep, PD.ap()[:, 0:E1], wait=[("pe", mm1)])
        p.used(("pdrep",), "dve", evr)
        TT(ge.rearrange("p (c e) -> p c e", e=E1),
           R_re.rearrange("p (c e) -> p c e", e=E1),
           mrep.unsqueeze(1).broadcast_to([128, 128, E1]), ALU.is_ge,
           wait=[("dq_sync", crall)])
        RED(cntpc, ge.rearrange("p (c e) -> p e c", e=E1), ALU.add)
        mm2 = MM(PC.ap()[0:1, 0:E1], ones_col, cntpc, start=True, stop=True,
                 wait=[("dve", p.now("dve"))] + p.need(("pccnt",)))
        cg = TS(cgt, PC.ap()[0:1, 0:E1], 2048.0, None, ALU.is_gt,
                wait=[("pe", mm2)])
        p.used(("pccnt",), "dve", cg)
        dlo = T12.ap()[0:1, 680:688]
        dhi = T12.ap()[0:1, 688:696]
        TT(dlo, mid, lo, ALU.subtract)
        TT(dlo, dlo, cgt, ALU.mult)
        TT(lo, lo, dlo, ALU.add)
        TT(dhi, mid, hi, ALU.subtract)
        ncg = T12.ap()[0:1, 696:704]
        TS(ncg, cgt, -1.0, 1.0, ALU.mult, ALU.add)
        TT(dhi, dhi, ncg, ALU.mult)
        TT(hi, hi, dhi, ALU.add)
    tau_done = p.now("dve")
    p.used(("mt9", 0), "dve", tau_done)
    p.used(("mt9", 1), "dve", tau_done)
    mm3 = MM(PD.ap()[:, 0:E1], ones_row, hi, start=True, stop=True,
             wait=p.need(("pdrep",)) + [("dve", tau_done)])
    trep = T12.ap()[:, 800:808]
    evt = CP(trep, PD.ap()[:, 0:E1], wait=[("pe", mm3)])
    keep = T12.ap()[:, 816:824]
    for j in range(NT):
        TT(keep, Rm[:, j * E1:(j + 1) * E1],
           trep, ALU.is_ge, wait=[("dq_sync", cpm), ("dve", evt)])
        TT(sel_tm[:, j * E1:(j + 1) * E1], pm[:, j * E1:(j + 1) * E1], keep, ALU.mult)
    sel_done = p.now("dve")
    if "sel" in debug:
        p.dma("gpsimd", dbg_out("sel", [128, NT * E1]), sel_tm,
              wait=[("dve", sel_done)])

    for e in range(E1):
        for isl in range(NS):
            wslot = isl % 2
            st1 = T9.ap()[:, wslot * 1024:(wslot + 1) * 1024]
            w1c = T10.ap()[:, wslot * 1024:(wslot + 1) * 1024]
            cd1 = p.dma("sync", st1.rearrange("p (c n) -> p c n", n=128),
                        w1T[e][:, isl * 128:(isl + 1) * 128].rearrange(
                            "(c p) n -> p c n", p=128),
                        wait=p.need(("mt9", wslot)))
            cw1 = CP(w1c, st1, wait=[("dq_sync", cd1)] + p.need(("mw1", wslot)))
            p.used(("mt9", wslot), "dve", cw1)
            for jc in range(NJ):
                ph = PC.ap() if jc % 2 == 0 else PD.ap()
                w0 = [("dve", cw1), ("dq_sync", cxn)] + p.need(("ph", jc % 2))
                for kc in range(NS):
                    l1 = MM(ph, w1c[:, kc * 128:(kc + 1) * 128],
                            T3.ap()[:, kc * T + jc * 512: kc * T + (jc + 1) * 512],
                            start=(kc == 0), stop=(kc == NS - 1),
                            wait=(w0 if kc == 0 else ()))
                hdst = T2.ap()[:, isl * T + jc * 512: isl * T + (jc + 1) * 512]
                hd = ACTF(hdst, ph, AF.Gelu,
                          wait=[("pe", l1)] + p.need(("hbuf2", isl, jc)))
                p.used(("ph", jc % 2), "act", hd)
            p.used(("mw1", wslot), "pe", p.now("pe"))
        h_done = p.now("act")
        h_pe = p.now("pe")
        w2loc = []
        for isl in range(NS):
            wslot = isl % 2
            st1 = T9.ap()[:, wslot * 1024:(wslot + 1) * 1024]
            dstw = (T10.ap() if isl < 4 else TK.ap())[:, (isl % 4) * 1024:
                                                       ((isl % 4) + 1) * 1024]
            w2loc.append(dstw)
            cd2 = p.dma("sync", st1, w2T[e][isl * 128:(isl + 1) * 128, :],
                        wait=p.need(("mt9", wslot)))
            cw2 = CP(dstw, st1, wait=[("dq_sync", cd2), ("pe", h_pe)] +
                     p.need(("mw2", isl)))
            p.used(("mt9", wslot), "dve", cw2)
        w2_done = p.now("dve")
        for j in range(NT):
            ps = PA.ap()[:, (j % 2) * 1024:(j % 2 + 1) * 1024]
            w0 = [("dve", w2_done), ("act", h_done), ("dve", sel_done),
                  ("dq_sync", cx)] + p.need(("pay2", j % 2))
            for isl in range(NS):
                for n in range(2):
                    last = MM(ps[:, n * 512:(n + 1) * 512],
                              T2.ap()[:, isl * T + j * 128: isl * T + (j + 1) * 128],
                              w2loc[isl][:, n * 512:(n + 1) * 512],
                              start=(isl == 0), stop=(isl == NS - 1),
                              wait=(w0 if (isl == 0 and n == 0) else ()))
            dst = X1.ap()[:, j * 1024:(j + 1) * 1024]
            ac = STT(dst, ps, sel_tm[:, j * E1 + e: j * E1 + e + 1], dst,
                     ALU.mult, ALU.add, wait=[("pe", last)])
            p.used(("pay2", j % 2), "dve", ac)
        for isl in range(NS):
            p.used(("mw2", isl), "pe", p.now("pe"))
            for jc in range(NJ):
                p.used(("hbuf2", isl, jc), "pe", p.now("pe"))
    done = p.now("dve")
    p.dma("gpsimd", out.rearrange("(j p) d -> p j d", p=128),
          X1.ap().rearrange("p (j d) -> p j d", d=1024), wait=[("dve", done)])
    p.barrier()
    p.build(stk)
    return nc, p, stk, dbg


def prep_B(inputs, core, partial, xn1f, R_stack, R_my, p_my):
    g = lambda n: np.asarray(inputs[n])
    m = dict(partial=partial, xn1f=xn1f, R_all=R_stack, R_my=R_my, p_my=p_my,
             w1T=_np_t(g("L1_ffn_w1")), w2T=_np_t(g("L1_ffn_w2")))
    return {k: np.ascontiguousarray(np.asarray(v)) for k, v in m.items()}


def kernel(**inputs):
    ncores = 8
    ncA, pA, stkA, _ = build_progA(upto="all", debug=())
    mapsA = [prep_A(inputs, c) for c in range(ncores)]
    resA = run_bass_kernel_spmd(ncA, mapsA, core_ids=list(range(ncores)))
    stkA.close()
    R_stack = np.concatenate([resA.results[c]["R_out"] for c in range(ncores)], 0)
    ncB, pB, stkB, _ = build_progB()
    mapsB = [prep_B(inputs, c, resA.results[c]["partial"], resA.results[c]["xn1f"],
                    R_stack, resA.results[c]["R_out"], resA.results[c]["p_out"])
             for c in range(ncores)]
    resB = run_bass_kernel_spmd(ncB, mapsB, core_ids=list(range(ncores)))
    stkB.close()
    return np.stack([resB.results[c]["out_f"] for c in range(ncores)]).astype(np.float32)
